# version 04
# baseline (speedup 1.0000x reference)
import numpy as np
import ml_dtypes
import contextlib

import concourse.bass as bass
import concourse.mybir as mybir
from concourse.bass_utils import run_bass_kernel_spmd

F32 = mybir.dt.float32
BF16 = mybir.dt.bfloat16
F16 = mybir.dt.float16
U8 = mybir.dt.uint8
AF = mybir.ActivationFunctionType
ALU = mybir.AluOpType
AX = mybir.AxisListType
BF = ml_dtypes.bfloat16

T = 2048
C = 2048
NH = 16
NKV = 4
HD = 128
NE = 16
HH = 512
NCH = 16
NELEM = 2
LN_EPS = 1e-5
L2_EPS = 1e-12
KB = 1024
PROG_SEMS = ["T", "S", "V", "Pq"]
OWN = {"tensor": "T", "scalar": "S", "vector": "V", "gpsimd": "Pq", "sync": None}


def host_constants():
    d = HD
    inv_freq = 1.0 / (10000.0 ** (np.arange(0, d, 2, dtype=np.float64) / d))
    t = np.arange(T, dtype=np.float64)
    freqs = t[:, None] * inv_freq[None, :]
    emb = np.concatenate([freqs, freqs], axis=-1)
    s4 = float(d) ** -0.25
    cs = np.stack([np.asarray((np.cos(emb).T * s4).astype(np.float16)),
                   np.asarray((np.sin(emb).T * s4).astype(np.float16))])
    tri = (np.arange(128)[:, None] <= np.arange(128)[None, :]).astype(BF)
    return dict(
        cs=np.ascontiguousarray(cs),
        tri=np.ascontiguousarray(tri),
        ident=np.eye(128, dtype=BF),
        ones_c=np.ones((128, 1), dtype=BF),
        twos_c=np.full((128, 1), 2.0, dtype=BF),
        ones_rb=np.ones((1, 128), dtype=BF),
        ones_rf=np.ones((1, 128), dtype=np.float32),
        one_f=np.ones((1, 1), dtype=np.float32),
    )


class B:
    def __init__(self):
        self.events = []
        self.counts = {s: 0 for s in PROG_SEMS}
        self.tok = {}

    def op(self, eng, fn, inc=()):
        for s, a in inc:
            self.counts[s] = self.counts.get(s, 0) + a
        self.events.append((eng, "op", fn, tuple(inc)))

    def wait(self, eng, sem, val):
        if val and val > 0:
            self.events.append((eng, "wait", sem, val))

    def c(self, sem):
        return self.counts.get(sem, 0)

    def mark(self, name, *semlist):
        d = self.tok.setdefault(name, {})
        for s in semlist:
            d[s] = max(d.get(s, 0), self.c(s))

    def set_tok(self, name, **kw):
        self.tok[name] = dict(kw)

    def need(self, eng, *names):
        for name in names:
            for s, n in self.tok.get(name, {}).items():
                self.wait(eng, s, n)

    def barrier(self):
        bar = {s: self.c(s) for s in self.counts}
        for eng in ("sync", "tensor", "scalar", "vector", "gpsimd"):
            own = OWN[eng]
            for s, n in bar.items():
                if s != own and n > 0:
                    self.wait(eng, s, n)
        for name in list(self.tok):
            d = {s: n for s, n in self.tok[name].items() if n > bar[s]}
            if d:
                self.tok[name] = d
            else:
                del self.tok[name]

    def replay(self, eng_name, eng, sems):
        for ev in self.events:
            if ev[0] != eng_name:
                continue
            if ev[1] == "op":
                inst = ev[2](eng)
                for s, a in ev[3]:
                    inst.then_inc(sems[s], a)
            else:
                eng.wait_ge(sems[ev[2]], ev[3])


def build_program(lite=False, debug=False, phases=7, nelem=NELEM, a0stop=9, ropemode=2):
    nc = bass.Bass("TRN2", target_bir_lowering=False)

    d_x = nc.dram_tensor("x", [NELEM, T, C], F32, kind="ExternalInput").ap()
    d_xT = nc.dram_tensor("xT", [NELEM, C, T], F32, kind="ExternalInput").ap()
    d_wq = nc.dram_tensor("wq_r", [NH, 128, C], F32, kind="ExternalInput").ap()
    d_wk = nc.dram_tensor("wk", [C, NKV * HD], F32, kind="ExternalInput").ap()
    d_wv = nc.dram_tensor("wv", [C, NKV * HD], F32, kind="ExternalInput").ap()
    d_wo = nc.dram_tensor("wo", [C, C], F32, kind="ExternalInput").ap()
    d_mu = nc.dram_tensor("mu_r", [NE, 128, C], F32, kind="ExternalInput").ap()
    d_w1 = nc.dram_tensor("W1", [NE, C, HH], F32, kind="ExternalInput").ap()
    d_w2 = nc.dram_tensor("W2", [NE, HH, C], F32, kind="ExternalInput").ap()
    d_b1 = nc.dram_tensor("b1", [NE, HH], F32, kind="ExternalInput").ap()
    d_b2 = nc.dram_tensor("b2", [NE, C], F32, kind="ExternalInput").ap()
    d_gam = nc.dram_tensor("gamma", [1, C], F32, kind="ExternalInput").ap()
    d_bet = nc.dram_tensor("beta", [1, C], F32, kind="ExternalInput").ap()
    d_scale = nc.dram_tensor("scale", [1, 1], F32, kind="ExternalInput").ap()
    d_cs = nc.dram_tensor("cs", [2, 128, T], F16, kind="ExternalInput").ap()
    d_tri = nc.dram_tensor("tri", [128, 128], BF16, kind="ExternalInput").ap()
    d_id = nc.dram_tensor("ident", [128, 128], BF16, kind="ExternalInput").ap()
    d_onc = nc.dram_tensor("ones_c", [128, 1], BF16, kind="ExternalInput").ap()
    d_two = nc.dram_tensor("twos_c", [128, 1], BF16, kind="ExternalInput").ap()
    d_onrb = nc.dram_tensor("ones_rb", [1, 128], BF16, kind="ExternalInput").ap()
    d_onrf = nc.dram_tensor("ones_rf", [1, 128], F32, kind="ExternalInput").ap()
    d_onef = nc.dram_tensor("one_f", [1, 1], F32, kind="ExternalInput").ap()

    d_out = nc.dram_tensor("out", [NELEM, T, C], F32, kind="ExternalOutput").ap()
    spkind = "ExternalOutput" if debug else "Internal"
    d_x1sp = nc.dram_tensor("x1sp", [T, C], F32, kind=spkind).ap()
    d_xlnsp = nc.dram_tensor("xlnsp", [T, C], BF16, kind=spkind).ap()
    if debug:
        d_dbgb = nc.dram_tensor("dbgb", [20, 128, 2048], BF16, kind="ExternalOutput").ap()
    dbg_i = [0]

    HEADS = list(range(2 if lite else NH))
    EXPS = list(range(2 if lite else NE))

    st = contextlib.ExitStack()
    with st:
        E = st.enter_context
        TOTAL = 212800
        AR = E(nc.sbuf_tensor("AR", [128, TOTAL], U8)).ap()

        def v(off, dtype, cols, rows=128):
            esz = {F32: 4, BF16: 2, F16: 2}[dtype]
            assert off + cols * esz <= TOTAL, (off, cols)
            ap = AR[:, off:off + cols * esz].bitcast(dtype)
            if rows != 128:
                ap = ap[0:rows, :]
            return ap

        R0 = 0
        R1 = R0 + 65536
        R2 = R1 + 65536
        R3 = R2 + 32768
        R4 = R3 + 12288
        R5 = R4 + 8192
        R6 = R5 + 8192
        R7 = R6 + 5696
        R8 = R7 + 8192
        SM = R8 + 4096
        assert SM + 2304 <= TOTAL

        sm_tri = v(SM + 0, BF16, 128)
        sm_id = v(SM + 256, BF16, 128)
        sm_onc = v(SM + 512, BF16, 1)
        sm_two = v(SM + 576, BF16, 1)
        sm_onrb = v(SM + 640, BF16, 128, rows=1)
        sm_onrf = v(SM + 896, F32, 128, rows=1)
        sm_onef = v(SM + 1408, F32, 1, rows=1)
        sm_negS = v(SM + 1472, F32, 1)
        sm_scol = v(SM + 1536, F32, 1)
        sm_r = v(SM + 1600, F32, 16)
        sm_csi = v(SM + 1664, F32, 16)
        sm_rsi = v(SM + 1728, F32, 16)
        sm_csc = v(SM + 1792, F32, 16)
        sm_st = v(SM + 1856, F32, 16)
        sm_sc1 = v(SM + 1920, F32, 16)
        sm_csp = v(SM + 1984, F32, 64)

        xTv = [v(R0 + cc * 4 * KB, BF16, 2048) for cc in range(NCH)]
        wob = [v(R0 + cc * 4 * KB, BF16, 2048) for cc in range(NCH)]
        eTv = [v(R0 + i * 4 * KB, BF16, 2048) for i in range(NE)]
        wkb = [v(R1 + cc * KB, BF16, 512) for cc in range(NCH)]
        wkb_slab = [v(R1 + q * 4 * KB, BF16, 2048) for q in range(4)]
        wvb = [v(R1 + 16 * KB + cc * KB, BF16, 512) for cc in range(NCH)]
        wvb_slab = [v(R1 + 16 * KB + q * 4 * KB, BF16, 2048) for q in range(4)]
        oTv = v(R1, BF16, 32768)
        xnTr = v(R1, BF16, 16384).rearrange("p (dc n) -> p dc n", n=1024)
        xlnv = [v(R1 + i * 4 * KB, BF16, 2048) for i in range(NCH)]
        soutv = [v(R1 + i * 4 * KB, BF16, 2048) for i in range(NE)]
        kTv = [v(R2 + kv * 4 * KB, BF16, 2048) for kv in range(NKV)]
        vtv = [v(R2 + 16 * KB + tc * KB, BF16, 512) for tc in range(NCH)]
        xlnb = [v(R2 + i * 4 * KB, BF16, 2048) for i in range(2)]
        munb = v(R2 + 16 * KB, BF16, 2048)
        Hv = [v(R2 + e_ * KB, BF16, 512) for e_ in range(NE)]
        Eev = [v(R2 + 16 * KB + n_ * 256, BF16, 128) for n_ in range(NCH)]
        siv = v(R2 + 20 * KB, BF16, 2048)
        siTv = [v(R2 + 24 * KB + dc * 256, BF16, 128) for dc in range(NCH)]
        hTv = [v(R2 + 28 * KB + hc * 256, BF16, 128) for hc in range(4)]
        ebuf = [v(R3 + j * KB, BF16, 512) for j in range(8)]
        b2rv = v(R3, BF16, 2048)
        b1rv = v(R3 + 4 * KB, BF16, 512)
        rrow = [v(R3 + i * 2 * KB, F32, 512, rows=1) for i in range(2)]
        gamrep = v(R3, BF16, 2048)
        betrep = v(R3 + 4 * KB, BF16, 2048)
        row_a = v(R3, F32, 2048, rows=1)
        row_bf = v(R3, F32, 512, rows=1)
        row_bb = v(R3 + 8 * KB, BF16, 2048, rows=1)
        wst = [v(R4 + i * 4 * KB, F32, 1024) for i in range(2)]
        wsb = [v(R5 + i * 4 * KB, BF16, 2048) for i in range(2)]
        dscr = [v(R6 + i * 256, BF16, 128) for i in range(2)]
        elscr = [v(R6 + 512 + i * 256, BF16, 128) for i in range(4)]
        rope_rot = v(R6, BF16, 512)
        olbv = v(R6 + 1536, BF16, 512)
        rope_t1 = v(R6 + 2560, BF16, 512)
        rope_t2 = v(R6 + 3584, BF16, 512)
        ogbv = v(R6 + 4608, BF16, 512)
        musq = v(R6, BF16, 2048)
        repb = v(R6 + 4096, BF16, 128)
        irow = v(R6 + 4352, F32, 128, rows=1)
        irow2 = v(R6 + 4864, F32, 128, rows=1)
        dumpb = v(R6, BF16, 2048)
        cosv = v(R7, F16, 2048)
        sinv = v(R7 + 4 * KB, F16, 2048)
        xrc = v(R7, F32, 2048)
        qsv = v(R8, BF16, 2048)
        xnc = v(R8, BF16, 2048)

        PB = [E(nc.psum_tensor(f"pb{i}", [128, 512], F32)) for i in range(8)]
        PBN = {id(PB[i]): f"pb{i}" for i in range(8)}

        p = B()
        rr = [0]

        def dma_in(dst, src, bufname, eng="sync", sem=None):
            sem = sem or ("D_" + bufname)
            p.need(eng, bufname + ".free")
            p.op(eng, lambda e, d=dst, s=src: e.dma_start(out=d, in_=s),
                 inc=[(sem, 16)])
            p.set_tok(bufname + ".rdy", **{sem: p.c(sem)})
            return p.c(sem)

        def dma_out(dst, src, sem):
            p.op("scalar", lambda e, d=dst, s=src: e.dma_start(out=d, in_=s),
                 inc=[(sem, 16)])
            return p.c(sem)

        def stream2(dram_ap, dst_bf, nm):
            three_d = len(dram_ap.shape) == 3
            toks = {}
            for hf in range(2):
                sl = rr[0] % 2
                ceng = "gpsimd" if sl == 0 else "vector"
                csem = OWN[ceng]
                rr[0] += 1
                if three_d:
                    src = dram_ap[:, 2 * hf:2 * hf + 2, :]
                    dst_st = wst[sl].rearrange("p (a n) -> p a n", n=dram_ap.shape[2])
                else:
                    src = dram_ap[:, hf * 1024:(hf + 1) * 1024]
                    dst_st = wst[sl]
                dma_in(dst_st, src, f"wst{sl}")
                p.need(ceng, f"wst{sl}.rdy", nm + ".free")
                p.op(ceng, lambda e, d=dst_bf[:, hf * 1024:(hf + 1) * 1024], s=wst[sl]:
                     e.tensor_copy(out=d, in_=s), inc=[(csem, 1)])
                p.mark(f"wst{sl}.free", csem)
                toks[csem] = p.c(csem)
            p.set_tok(nm + ".rdy", **toks)
            return toks

        def mm(out_ap, lhsT, rhs, start, stop, inc=False):
            p.op("tensor", lambda e, o=out_ap, l=lhsT, r=rhs, s=start, t=stop:
                 e.matmul(out=o, lhsT=l, rhs=r, start=s, stop=t),
                 inc=[("T", 1)] if inc else ())
            return p.c("T")

        def bnm(b):
            return PBN[id(b)]

        def sc_act(out, in_, func, bias=0.0, scale=1.0, accum=None, inc=True):
            p.op("scalar", lambda e, o=out, i=in_, f=func, bi=bias, sc=scale, a=accum:
                 (e.activation(out=o, in_=i, func=f, bias=bi, scale=sc, accum_out=a)
                  if a is not None else
                  e.activation(out=o, in_=i, func=f, bias=bi, scale=sc)),
                 inc=[("S", 1)] if inc else ())
            return p.c("S")

        def vt_op(fn, inc=True):
            p.op("vector", fn, inc=[("V", 1)] if inc else ())
            return p.c("V")

        def ch(eng):
            p.wait(eng, OWN[eng], p.c(OWN[eng]))

        def dbg(src, cols=2048):
            if not debug:
                return
            i = dbg_i[0]
            dbg_i[0] += 1
            if i >= 20:
                return
            p.op("gpsimd", lambda e, s=src, i=i, c=cols: e.dma_start(
                out=d_dbgb[i][:, 0:c], in_=s), inc=[("D_dbg", 16)])

        def rope_evac(b, dst, tg, tmm):
            bap = b.ap()
            cseg = cosv[:, tg * 512:(tg + 1) * 512]
            sseg = sinv[:, tg * 512:(tg + 1) * 512]
            if ropemode == 0:
                p.wait("scalar", "T", tmm)
                s0 = sc_act(dst, bap, AF.Copy)
                p.set_tok(bnm(b) + ".free", S=s0)
                return
            if ropemode == 1:
                p.wait("scalar", "T", tmm)
                p.need("scalar", "ropescr.free")
                p.op("scalar", lambda e: e.activation(out=rope_rot[0:64, :], in_=bap[64:128, :],
                                                      func=AF.Copy, bias=0.0, scale=-1.0))
                p.op("scalar", lambda e: e.copy(out=rope_rot[64:128, :], in_=bap[0:64, :]))
                s0 = sc_act(dst, bap, AF.Copy)
                p.set_tok("ropescr.free", S=s0)
                p.set_tok(bnm(b) + ".free", S=s0)
                return
            p.wait("scalar", "T", tmm)
            p.need("scalar", "ropescr.free")
            p.op("scalar", lambda e: e.activation(out=rope_rot[0:64, :], in_=bap[64:128, :],
                                                  func=AF.Copy, bias=0.0, scale=-1.0))
            p.op("scalar", lambda e: e.copy(out=rope_rot[64:128, :], in_=bap[0:64, :]),
                 inc=[("S", 1)])
            srot = p.c("S")
            p.wait("vector", "T", tmm)
            p.need("vector", "ropescr.free")
            p.wait("vector", "S", srot)
            vt_op(lambda e: e.tensor_tensor(out=rope_t1, in0=bap, in1=cseg, op=ALU.mult), inc=False)
            vt_op(lambda e: e.tensor_tensor(out=rope_t2, in0=rope_rot, in1=sseg, op=ALU.mult))
            ch("vector")
            vt_op(lambda e, d=dst: e.tensor_tensor(out=d, in0=rope_t1, in1=rope_t2, op=ALU.add))
            p.set_tok("ropescr.free", V=p.c("V"))
            p.set_tok(bnm(b) + ".free", V=p.c("V"), S=srot)

        def stats_ln(ybuf, sum_col):
            i = sum_col
            p.need("scalar", "dumpb.free")
            ssq = sc_act(dumpb, ybuf, AF.Square, accum=sm_st[:, i + 1:i + 2])
            p.set_tok("dumpb.free", S=ssq)
            p.wait("vector", "S", ssq)
            vt_op(lambda e: e.tensor_scalar(out=sm_st[:, i + 2:i + 3], in0=sm_st[:, i:i + 1],
                                            scalar1=1.0 / 2048, scalar2=None, op0=ALU.mult))
            vt_op(lambda e: e.tensor_scalar(out=sm_st[:, i + 4:i + 5], in0=sm_st[:, i + 1:i + 2],
                                            scalar1=1.0 / 2048, scalar2=None, op0=ALU.mult))
            ch("vector")
            vt_op(lambda e: e.tensor_tensor(out=sm_st[:, i + 3:i + 4], in0=sm_st[:, i + 2:i + 3],
                                            in1=sm_st[:, i + 2:i + 3], op=ALU.mult))
            ch("vector")
            vt_op(lambda e: e.tensor_tensor(out=sm_st[:, i + 5:i + 6],
                                            in0=sm_st[:, i + 4:i + 5],
                                            in1=sm_st[:, i + 3:i + 4], op=ALU.subtract))
            ch("vector")
            vvar = vt_op(lambda e: e.tensor_scalar(out=sm_st[:, i + 5:i + 6],
                                                   in0=sm_st[:, i + 5:i + 6],
                                                   scalar1=LN_EPS, scalar2=None, op0=ALU.add))
            p.wait("scalar", "V", vvar)
            sstd = sc_act(sm_st[:, i + 6:i + 7], sm_st[:, i + 5:i + 6], AF.Sqrt)
            p.wait("vector", "S", sstd)
            vt_op(lambda e: e.reciprocal(out=sm_st[:, i + 7:i + 8],
                                         in_=sm_st[:, i + 6:i + 7]))
            ch("vector")
            vstat = vt_op(lambda e: e.tensor_scalar(out=sm_st[:, i + 8:i + 9],
                                                    in0=sm_st[:, i + 2:i + 3],
                                                    scalar1=sm_st[:, i + 7:i + 8],
                                                    scalar2=-1.0, op0=ALU.mult, op1=ALU.mult))
            p.wait("scalar", "V", vstat)
            return sm_st[:, i + 7:i + 8], sm_st[:, i + 8:i + 9]

        for dst, src in ((sm_tri, d_tri), (sm_id, d_id), (sm_onc, d_onc),
                         (sm_two, d_two), (sm_onrb, d_onrb), (sm_onrf, d_onrf),
                         (sm_onef, d_onef)):
            p.op("sync", lambda e, d=dst, s=src: e.dma_start(out=d, in_=s), inc=[("D_const", 16)])
        p.op("sync", lambda e: e.dma_start(out=row_bf[:, 0:1], in_=d_scale), inc=[("D_const", 16)])
        cq = p.c("D_const")
        p.wait("tensor", "D_const", cq)
        mm(PB[7].ap()[:, 0:1], sm_onrf, row_bf[:, 0:1], True, True, inc=True)
        p.wait("vector", "T", p.c("T"))
        vt_op(lambda e: e.tensor_copy(out=sm_scol, in_=PB[7].ap()[:, 0:1]))
        ch("vector")
        vt_op(lambda e: e.tensor_scalar(out=sm_st[:, 0:1], in0=sm_scol,
                                        scalar1=-1.0, scalar2=None, op0=ALU.mult))
        ch("vector")
        vns = vt_op(lambda e: e.tensor_tensor(out=sm_negS, in0=sm_scol,
                                              in1=sm_st[:, 0:1], op=ALU.min))
        p.set_tok("pb7.free", V=vns)
        p.mark("consts.rdy", "V", "D_const")

        if lite:
            p.op("gpsimd", lambda e: e.memset(oTv, 0.0))
            p.op("gpsimd", lambda e: e.memset(v(R2, BF16, 8192), 0.0))
            p.op("gpsimd", lambda e: e.memset(v(R2 + 16 * KB, BF16, 8192), 0.0),
                 inc=[("Pq", 1)])
            p.mark("litezero.rdy", "Pq")

        for el in range(nelem):
            if el > 0:
                p.barrier()

            if a0stop < 1:
                continue
            dma_in(cosv, d_cs[0], "cosv", sem="D_cs")
            dma_in(sinv, d_cs[1], "sinv", sem="D_cs")
            cstok = p.c("D_cs")
            for cc in range(NCH):
                stream2(d_xT[el, cc * 128:(cc + 1) * 128, :], xTv[cc], f"xT{cc}")
            if a0stop < 2:
                continue
            for nm, dsrc, slabs in (("wk", d_wk, wkb_slab), ("wv", d_wv, wvb_slab)):
                for q in range(4):
                    ap = dsrc[q * 512:(q + 1) * 512, :].rearrange("(a p) n -> p a n", p=128)
                    stream2(ap, slabs[q], f"{nm}{q}")
                p.set_tok(nm + ".rdy", Pq=p.c("Pq"), V=p.c("V"))

            if a0stop < 3:
                continue
            import os
            KT = int(os.environ.get("KTILES", 16))
            for kv in range(NKV):
                for tg in range(4):
                    if kv * 4 + tg >= KT:
                        continue
                    b = PB[(kv * 4 + tg) % 2]
                    p.need("tensor", bnm(b) + ".free", "wk.rdy")
                    if kv == 0 and tg == 0:
                        p.wait("tensor", "D_cs", cstok)
                        for cc in range(NCH):
                            p.need("tensor", f"xT{cc}.rdy")
                    for cc in range(NCH):
                        mm(b.ap(), wkb[cc][:, kv * 128:(kv + 1) * 128],
                           xTv[cc][:, tg * 512:(tg + 1) * 512],
                           cc == 0, cc == NCH - 1, inc=(cc == NCH - 1))
                    rope_evac(b, kTv[kv][:, tg * 512:(tg + 1) * 512], tg, p.c("T"))
            if a0stop < 4:
                continue
            for tc in range(NCH):
                b = PB[4 + tc % 2]
                p.need("tensor", bnm(b) + ".free", "wv.rdy")
                for cc in range(NCH):
                    mm(b.ap(), xTv[cc][:, tc * 128:(tc + 1) * 128], wvb[cc],
                       cc == 0, cc == NCH - 1, inc=(cc == NCH - 1))
                p.wait("scalar", "T", p.c("T"))
                sv = sc_act(vtv[tc], b.ap(), AF.Copy)
                p.set_tok(bnm(b) + ".free", S=sv)
            p.mark("vt.rdy", "S")

            for h in (HEADS if phases >= 2 else []):
                kv = h // 4
                stream2(d_wq[h], wsb[0], "wsbq")
                for tg in range(4):
                    b = PB[6]
                    p.need("tensor", "pb6.free", "wsbq.rdy", "qs.free")
                    for cc in range(NCH):
                        mm(b.ap(), wsb[0][:, cc * 128:(cc + 1) * 128],
                           xTv[cc][:, tg * 512:(tg + 1) * 512],
                           cc == 0, cc == NCH - 1, inc=(cc == NCH - 1))
                    tmm = p.c("T")
                    p.need("scalar", "qs.free")
                    p.need("vector", "qs.free")
                    rope_evac(b, qsv[:, tg * 512:(tg + 1) * 512], tg, tmm)
                    p.set_tok("pb6.free", **p.tok[bnm(b) + ".free"])
                p.set_tok("qs.rdy", V=p.c("V"), S=p.c("S"))
                p.set_tok("ropescr.used", V=p.c("V"), S=p.c("S"))
                p.tok.pop("qs.free", None)
                p.mark("wsbq.free", "T")

                for g in range(4):
                    jmax = 4 * g + 3
                    etok = {}
                    p.need("tensor", "pb2.free", "pb4.free", "vt.rdy")

                    def emit_pv(j, g=g, kv=kv):
                        pp = j - 4 * g
                        c0 = max(0, pp) * 128
                        stk, vtk = etok[j]
                        p.wait("tensor", "S", stk)
                        if vtk:
                            p.wait("tensor", "V", vtk)
                        mm(PB[2].ap()[:, c0:512], vtv[j][:, kv * 128:(kv + 1) * 128],
                           ebuf[j % 8][:, c0:512], j == 0, j == jmax)
                        mm(PB[4].ap()[0:1, c0:512], sm_two, ebuf[j % 8][:, c0:512],
                           j == 0, j == jmax, inc=True)
                        p.set_tok(f"eb{j % 8}.free", T=p.c("T"))

                    for j in range(jmax + 1):
                        pp = j - 4 * g
                        c0 = max(0, pp) * 128
                        b = PB[j % 2]
                        p.need("tensor", bnm(b) + ".free")
                        if j == 0:
                            p.need("tensor", "qs.rdy")
                        mm(b.ap()[:, c0:512], kTv[kv][:, j * 128:(j + 1) * 128],
                           qsv[:, g * 512 + c0:(g + 1) * 512], True, True, inc=True)
                        tmm = p.c("T")
                        if j >= 1:
                            emit_pv(j - 1)
                        p.wait("scalar", "T", tmm)
                        p.need("scalar", f"eb{j % 8}.free")
                        vtk = 0
                        if pp >= 0:
                            ds = dscr[j % 2]
                            p.need("scalar", f"ds{j % 2}.free", "ropescr.used")
                            sdiag = sc_act(ds, b.ap()[:, c0:c0 + 128], AF.Exp)
                            p.wait("vector", "S", sdiag)
                            p.need("vector", f"eb{j % 8}.free")
                            vtk = vt_op(lambda e, j=j, c0=c0, ds=ds: e.tensor_tensor(
                                out=ebuf[j % 8][:, c0:c0 + 128], in0=ds, in1=sm_tri,
                                op=ALU.mult))
                            p.set_tok(f"ds{j % 2}.free", V=vtk)
                            if c0 + 128 < 512:
                                sc_act(ebuf[j % 8][:, c0 + 128:512], b.ap()[:, c0 + 128:512],
                                       AF.Exp)
                        else:
                            sc_act(ebuf[j % 8], b.ap(), AF.Exp)
                        stk = p.c("S")
                        p.set_tok(bnm(b) + ".free",
                                  **({"S": stk, "V": vtk} if vtk else {"S": stk}))
                        etok[j] = (stk, vtk)
                    emit_pv(jmax)
                    eltile = {}
                    for isub in range(4):
                        i = 4 * g + isub
                        if i >= 2:
                            stk, _ = etok[i - 1]
                            p.wait("vector", "S", stk)
                            p.need("vector", f"el{isub}.free", "ropescr.used")
                            vtk = vt_op(lambda e, i=i, isub=isub: e.tensor_tensor(
                                out=elscr[isub],
                                in0=ebuf[(i - 1) % 8][:, isub * 128:(isub + 1) * 128],
                                in1=sm_tri, op=ALU.mult))
                            eltile[i] = (elscr[isub], vtk)
                    p.need("tensor", "pb3.free", "pb5.free")
                    for isub in range(4):
                        i = 4 * g + isub
                        cc0 = isub * 128
                        if i == 0:
                            parts = [(0, ebuf[0][:, 0:128])]
                        elif i == 1:
                            parts = [(0, ebuf[0][:, 128:256]), (1, ebuf[1][:, 128:256])]
                        else:
                            slr, vtk = eltile[i]
                            p.wait("tensor", "V", vtk)
                            if i % 2 == 0:
                                parts = [(i - 1, slr)]
                            else:
                                parts = [(i - 2, ebuf[(i - 2) % 8][:, cc0:cc0 + 128]), (i - 1, slr)]
                        for pi, (kchunk, rhs) in enumerate(parts):
                            lastp = pi == len(parts) - 1
                            mm(PB[3].ap()[:, cc0:cc0 + 128],
                               vtv[kchunk][:, kv * 128:(kv + 1) * 128], rhs, pi == 0, lastp)
                            mm(PB[5].ap()[0:1, cc0:cc0 + 128], sm_two, rhs, pi == 0, lastp,
                               inc=(isub == 3 and lastp))
                    tpv = p.c("T")
                    for isub in range(4):
                        if 4 * g + isub >= 2:
                            p.set_tok(f"el{isub}.free", T=tpv)
                    for j in range(max(0, jmax - 5), jmax + 1):
                        p.set_tok(f"eb{j % 8}.free", T=tpv)
                    p.wait("vector", "T", tpv)
                    p.need("vector", "eb0.free", "eb1.free", "eb2.free", "eb3.free")
                    vt_op(lambda e: e.reciprocal(out=rrow[0], in_=PB[4].ap()[0:1, :]), inc=False)
                    vrec = vt_op(lambda e: e.reciprocal(out=rrow[1], in_=PB[5].ap()[0:1, :]))
                    p.wait("scalar", "T", tpv)
                    p.need("scalar", "ogb.free")
                    sc_act(ogbv, PB[2].ap(), AF.Copy, inc=False)
                    sevac = sc_act(olbv, PB[3].ap(), AF.Copy)
                    p.wait("tensor", "S", sevac)
                    p.wait("tensor", "V", vrec)
                    mm(PB[2].ap(), sm_onrf, rrow[0], True, True)
                    trep = mm(PB[3].ap(), sm_onrf, rrow[1], True, True, inc=True)
                    for j in range(4):
                        p.set_tok(f"eb{j}.free", T=trep)
                    p.set_tok("pb4.free", V=vrec)
                    p.set_tok("pb5.free", V=vrec)
                    p.wait("vector", "T", trep)
                    vt_op(lambda e: e.tensor_tensor(out=ogbv, in0=ogbv, in1=PB[2].ap(),
                                                    op=ALU.mult), inc=False)
                    vt_op(lambda e: e.tensor_tensor(out=olbv, in0=olbv, in1=PB[3].ap(),
                                                    op=ALU.mult))
                    ch("vector")
                    vfin = vt_op(lambda e, h=h, g=g: e.tensor_tensor(
                        out=oTv[:, h * 2048 + g * 512:h * 2048 + (g + 1) * 512],
                        in0=ogbv, in1=olbv, op=ALU.add))
                    p.set_tok("pb2.free", V=vfin)
                    p.set_tok("pb3.free", V=vfin)
                    p.set_tok("ogb.free", V=vfin)
                p.set_tok("qs.free", T=p.c("T"))
                p.set_tok("ropescr.free", T=p.c("T"), V=p.c("V"), S=p.c("S"))
                for i2 in range(2):
                    p.mark(f"ds{i2}.free", "S", "V")
                for i2 in range(4):
                    p.mark(f"el{i2}.free", "T")
            p.mark("oT.rdy", "V")
            if phases < 3:
                continue
            if debug and el == 0:
                p.wait("gpsimd", "V", p.c("V"))
                dbg(kTv[0])
                dbg(vtv[0], 512)
                dbg(oTv[:, 0:2048])
            p.barrier()

            for cc in range(NCH):
                stream2(d_wo[cc * 128:(cc + 1) * 128, :], wob[cc], f"wob{cc}")
            for nmi, (dsrc, dst) in enumerate(((d_gam, gamrep), (d_bet, betrep))):
                for hf in range(2):
                    dma_in(wst[hf][0:1, :], dsrc[:, hf * 1024:(hf + 1) * 1024], f"wst{hf}",
                           sem="D_wstr")
                gq = p.c("D_wstr")
                p.wait("tensor", "D_wstr", gq)
                srow = 0
                for sg in range(4):
                    b = PB[sg % 2]
                    p.need("tensor", bnm(b) + ".free")
                    mm(b.ap(), sm_onrf, wst[sg // 2][0:1, (sg % 2) * 512:(sg % 2 + 1) * 512],
                       True, True, inc=True)
                    p.wait("vector", "T", p.c("T"))
                    srow = vt_op(lambda e, b=b, dst=dst, sg=sg: e.tensor_copy(
                        out=dst[:, sg * 512:(sg + 1) * 512], in_=b.ap()))
                    p.set_tok(bnm(b) + ".free", V=srow)
                p.set_tok("wst0.free", T=p.c("T"))
                p.set_tok("wst1.free", T=p.c("T"))
            p.mark("gbrep.rdy", "V")

            for tc in range(NCH):
                grp = [PB[4 * (tc % 2) + i] for i in range(4)]
                nx = dma_in(xrc, d_x[el, tc * 128:(tc + 1) * 128, :], "r7")
                for dg in range(4):
                    b = grp[dg]
                    p.need("tensor", bnm(b) + ".free")
                    if tc == 0 and dg == 0:
                        p.need("tensor", "oT.rdy", f"wob{NCH - 1}.rdy")
                    for cc in range(NCH):
                        mm(b.ap(), oTv[:, cc * 2048 + tc * 128:cc * 2048 + (tc + 1) * 128],
                           wob[cc][:, dg * 512:(dg + 1) * 512],
                           cc == 0, cc == NCH - 1, inc=(dg == 3 and cc == NCH - 1))
                tmm = p.c("T")
                p.wait("vector", "T", tmm)
                p.need("vector", "r7.rdy")
                for dg in range(4):
                    vy1 = vt_op(lambda e, dg=dg, grp=grp: e.scalar_tensor_tensor(
                        out=xrc[:, dg * 512:(dg + 1) * 512], in0=grp[dg].ap(), scalar=1.0,
                        in1=xrc[:, dg * 512:(dg + 1) * 512], op0=ALU.mult, op1=ALU.add,
                        accum_out=sm_sc1[:, dg:dg + 1]), inc=(dg == 3))
                for dg in range(4):
                    p.set_tok(bnm(grp[dg]) + ".free", V=vy1)
                ch("vector")
                vsum = vt_op(lambda e: e.tensor_reduce(out=sm_st[:, 0:1], in_=sm_sc1[:, 0:4],
                                                       axis=AX.X, op=ALU.add))
                p.wait("scalar", "V", vsum)
                rstd, bcol = stats_ln(xrc, 0)
                sx1 = sc_act(xrc, xrc, AF.Identity, bias=bcol, scale=rstd)
                ch("scalar")
                nsp = dma_out(d_x1sp[tc * 128:(tc + 1) * 128, :], xrc, "D_str7")
                p.wait("vector", "S", sx1)
                p.need("vector", "gbrep.rdy", "dumpb.free")
                vg = vt_op(lambda e: e.tensor_tensor(out=dumpb, in0=xrc, in1=gamrep, op=ALU.mult))
                p.set_tok("r7.free", V=vg, D_str7=nsp)
                sl2 = tc % 2
                p.need("vector", f"xlnb{sl2}.free")
                ch("vector")
                vxln = vt_op(lambda e, sl2=sl2: e.tensor_tensor(out=xlnb[sl2], in0=dumpb,
                                                                in1=betrep, op=ALU.add))
                p.set_tok("dumpb.free", V=vxln)
                p.wait("scalar", "V", vxln)
                p.need("scalar", "dumpb.free")
                sss = sc_act(dumpb, xlnb[sl2], AF.Square, accum=sm_st[:, 9:10])
                p.set_tok("dumpb.free", S=sss)
                ch("scalar")
                snrm = sc_act(sm_st[:, 10:11], sm_st[:, 9:10], AF.Sqrt)
                p.wait("vector", "S", snrm)
                vt_op(lambda e: e.tensor_scalar(out=sm_st[:, 11:12], in0=sm_st[:, 10:11],
                                                scalar1=L2_EPS, scalar2=None, op0=ALU.max))
                ch("vector")
                vt_op(lambda e: e.reciprocal(out=sm_st[:, 12:13], in_=sm_st[:, 11:12]))
                ch("vector")
                vt_op(lambda e, tc=tc: e.tensor_tensor(out=sm_r[:, tc:tc + 1],
                                                       in0=sm_st[:, 12:13], in1=sm_scol,
                                                       op=ALU.mult))
                nsp2 = dma_out(d_xlnsp[tc * 128:(tc + 1) * 128, :], xlnb[sl2], f"D_stxb{sl2}")
                p.set_tok(f"xlnb{sl2}.free", **{f"D_stxb{sl2}": nsp2})
            p.mark("r.rdy", "V")
            p.barrier()
            if phases < 4:
                continue

            for half in range(2):
                for nci in range(8):
                    n_ = half * 8 + nci
                    sl = nci % 2
                    dma_in(wsb[sl], d_xlnsp[n_ * 128:(n_ + 1) * 128, :], f"wsbF{sl}",
                           sem=f"D_wsbF{sl}")
                    p.need("scalar", f"wsbF{sl}.rdy", "xnc.free", "r.rdy")
                    sxn = sc_act(xnc, wsb[sl], AF.Copy, scale=sm_r[:, n_:n_ + 1])
                    p.set_tok(f"wsbF{sl}.free", S=sxn)
                    for f4 in range(4):
                        b = PB[6 + f4 % 2]
                        p.need("tensor", bnm(b) + ".free")
                        p.wait("tensor", "S", sxn)
                        for q4 in range(4):
                            dc = f4 * 4 + q4
                            mm(b.ap()[:, q4 * 128:(q4 + 1) * 128],
                               xnc[:, dc * 128:(dc + 1) * 128], sm_id,
                               True, True, inc=(q4 == 3))
                        p.wait("scalar", "T", p.c("T"))
                        p.need("scalar", "xnT.free")
                        sev = sc_act(xnTr[:, f4 * 4:f4 * 4 + 4, nci * 128:(nci + 1) * 128],
                                     b.ap(), AF.Copy)
                        p.set_tok(bnm(b) + ".free", S=sev)
                    p.set_tok("xnc.free", T=p.c("T"))
                for esc in range(NE):
                    slm = esc % 2
                    stream2(d_mu[esc], wsb[slm], f"wsbF{slm}")
                    p.need("vector", f"wsbF{slm}.rdy", "musq.free")
                    vsq = vt_op(lambda e, slm=slm: e.tensor_tensor(out=musq, in0=wsb[slm],
                                                                   in1=wsb[slm], op=ALU.mult))
                    p.need("tensor", "pb4.free")
                    p.wait("tensor", "V", vsq)
                    for dc in range(NCH):
                        mm(PB[4].ap()[0:1, 0:128], sm_onc, musq[:, dc * 128:(dc + 1) * 128],
                           dc == 0, dc == NCH - 1, inc=(dc == NCH - 1))
                    p.mark("musq.free", "T")
                    p.wait("vector", "T", p.c("T"))
                    p.need("vector", "irow.free")
                    v1 = vt_op(lambda e: e.tensor_copy(out=irow, in_=PB[4].ap()[0:1, 0:128]))
                    p.set_tok("pb4.free", V=v1)
                    p.wait("scalar", "V", v1)
                    s1 = sc_act(irow2, irow, AF.Sqrt)
                    p.wait("vector", "S", s1)
                    ch("vector") if False else None
                    vt_op(lambda e: e.tensor_scalar(out=irow2, in0=irow2, scalar1=L2_EPS,
                                                    scalar2=None, op0=ALU.max))
                    ch("vector")
                    vinv = vt_op(lambda e: e.reciprocal(out=irow2, in_=irow2))
                    p.set_tok("irow.free", V=vinv)
                    p.need("tensor", "pb5.free")
                    p.wait("tensor", "V", vinv)
                    trp = mm(PB[5].ap()[:, 0:128], sm_onrf, irow2, True, True, inc=True)
                    p.wait("vector", "T", trp)
                    p.need("vector", "repb.free")
                    srp = vt_op(lambda e: e.tensor_copy(out=repb, in_=PB[5].ap()[:, 0:128]))
                    p.set_tok("pb5.free", V=srp)
                    ch("vector")
                    p.need("vector", "munb.free")
                    vmn = vt_op(lambda e, slm=slm: e.tensor_tensor(
                        out=munb, in0=wsb[slm],
                        in1=repb.unsqueeze(1).broadcast_to([128, 16, 128]),
                        op=ALU.mult))
                    p.set_tok("repb.free", V=vmn)
                    p.set_tok(f"wsbF{slm}.free", V=vmn)
                    for ng2 in range(2):
                        b = PB[(esc % 2) * 2 + ng2]
                        p.need("tensor", bnm(b) + ".free")
                        p.wait("tensor", "V", vmn)
                        for dc in range(NCH):
                            mm(b.ap(), munb[:, dc * 128:(dc + 1) * 128],
                               xnTr[:, dc, ng2 * 512:(ng2 + 1) * 512],
                               dc == 0, dc == NCH - 1, inc=(dc == NCH - 1))
                        p.wait("scalar", "T", p.c("T"))
                        p.need("scalar", "consts.rdy")
                        ci = esc * 4 + half * 2 + ng2
                        sE = sc_act(eTv[esc][:, half * 1024 + ng2 * 512:
                                             half * 1024 + (ng2 + 1) * 512],
                                    b.ap(), AF.Exp, bias=sm_negS,
                                    accum=sm_csp[:, ci:ci + 1])
                        p.set_tok(bnm(b) + ".free", S=sE)
                    p.set_tok("munb.free", T=p.c("T"))
                p.set_tok("xnT.free", T=p.c("T"))
            p.mark("eT.rdy", "S")
            for ng in range(4):
                p.need("tensor", "pb5.free", "eT.rdy")
                for esc in range(NE):
                    mm(PB[5].ap()[0:1, 0:512], sm_onc, eTv[esc][:, ng * 512:(ng + 1) * 512],
                       esc == 0, esc == NE - 1, inc=(esc == NE - 1))
                p.wait("vector", "T", p.c("T"))
                p.need("vector", "rowa.free")
                srs = vt_op(lambda e, ng=ng: e.tensor_copy(
                    out=row_a[:, ng * 512:(ng + 1) * 512], in_=PB[5].ap()[0:1, 0:512]))
                p.set_tok("pb5.free", V=srs)
            srsall = p.c("V")
            p.need("tensor", "pb6.free")
            p.wait("tensor", "V", srsall)
            for n_ in range(NCH):
                mm(PB[6].ap()[:, n_:n_ + 1], row_a[:, n_ * 128:(n_ + 1) * 128],
                   sm_onef, True, True, inc=(n_ == NCH - 1))
            trs = p.c("T")
            p.set_tok("rowa.free", T=trs)
            p.wait("vector", "T", trs)
            vrs = vt_op(lambda e: e.reciprocal(out=sm_rsi, in_=PB[6].ap()[:, 0:16]))
            p.set_tok("pb6.free", V=vrs)
            p.wait("vector", "S", p.c("S"))
            for esc in range(NE):
                vt_op(lambda e, esc=esc: e.tensor_reduce(out=sm_csc[:, esc:esc + 1],
                                                         in_=sm_csp[:, esc * 4:esc * 4 + 4],
                                                         axis=AX.X, op=ALU.add),
                      inc=(esc == NE - 1))
            ch("vector")
            vcs = vt_op(lambda e: e.reciprocal(out=sm_csi, in_=sm_csc))
            p.mark("csi.rdy", "V")
            p.barrier()
            if phases < 5:
                continue

            for n_ in range(NCH):
                dma_in(xlnv[n_], d_xlnsp[n_ * 128:(n_ + 1) * 128, :], f"xlnv{n_}",
                       sem="D_xlnv")
            xlq = p.c("D_xlnv")
            for e_ in EXPS:
                for f4 in range(4):
                    b = PB[6 + f4 % 2]
                    p.need("tensor", bnm(b) + ".free", f"Ee{f4}.free")
                    for q4 in range(4):
                        n_ = f4 * 4 + q4
                        mm(b.ap()[:, q4 * 128:(q4 + 1) * 128],
                           eTv[e_][:, n_ * 128:(n_ + 1) * 128], sm_id,
                           True, True, inc=(q4 == 3))
                    p.wait("scalar", "T", p.c("T"))
                    sev = 0
                    for q4 in range(4):
                        sev = sc_act(Eev[f4 * 4 + q4], b.ap()[:, q4 * 128:(q4 + 1) * 128],
                                     AF.Copy, inc=(q4 == 3))
                    p.set_tok(bnm(b) + ".free", S=sev)
                    p.set_tok(f"Ee{f4}.rdy", S=sev)
                p.need("tensor", "pb0.free", "pb1.free", "pb2.free", "pb3.free")
                p.wait("tensor", "D_xlnv", xlq)
                for n_ in range(NCH):
                    p.need("tensor", f"Ee{n_ // 4}.rdy")
                    for dg in range(4):
                        mm(PB[dg].ap(), Eev[n_], xlnv[n_][:, dg * 512:(dg + 1) * 512],
                           n_ == 0, n_ == NCH - 1, inc=(n_ == NCH - 1 and dg == 3))
                tsi = p.c("T")
                for f4 in range(4):
                    p.set_tok(f"Ee{f4}.free", T=tsi)
                p.wait("scalar", "T", tsi)
                p.need("scalar", "csi.rdy", "si.free")
                ssi = 0
                for dg in range(4):
                    ssi = sc_act(siv[:, dg * 512:(dg + 1) * 512], PB[dg].ap(), AF.Copy,
                                 scale=sm_csi[:, e_:e_ + 1], inc=(dg == 3))
                for dg in range(4):
                    p.set_tok(f"pb{dg}.free", S=ssi)
                for f4 in range(4):
                    b = PB[6 + f4 % 2]
                    p.need("tensor", bnm(b) + ".free")
                    p.wait("tensor", "S", ssi)
                    for q4 in range(4):
                        dc = f4 * 4 + q4
                        mm(b.ap()[:, q4 * 128:(q4 + 1) * 128],
                           siv[:, dc * 128:(dc + 1) * 128], sm_id, True, True, inc=(q4 == 3))
                    p.wait("scalar", "T", p.c("T"))
                    sev = 0
                    for q4 in range(4):
                        sev = sc_act(siTv[f4 * 4 + q4], b.ap()[:, q4 * 128:(q4 + 1) * 128],
                                     AF.Copy, inc=(q4 == 3))
                    p.set_tok(bnm(b) + ".free", S=sev)
                stiT = p.c("S")
                bq = dma_in(row_bf, d_b1[e_:e_ + 1, :], "rowbf")
                p.need("tensor", "pb6.free")
                p.wait("tensor", "D_rowbf", bq)
                trb = mm(PB[6].ap(), sm_onrf, row_bf, True, True, inc=True)
                p.set_tok("rowbf.free", T=trb)
                p.wait("vector", "T", trb)
                p.need("vector", "b1rep.free")
                vbr1 = vt_op(lambda e: e.tensor_copy(out=b1rv, in_=PB[6].ap()))
                p.set_tok("pb6.free", V=vbr1)
                hb = PB[4 + e_ % 2]
                p.need("tensor", bnm(hb) + ".free")
                for qd in range(4):
                    ap = d_w1[e_, qd * 512:(qd + 1) * 512, :].rearrange("(a p) h -> p a h", p=128)
                    slw = qd % 2
                    stream2(ap, wsb[slw], f"wsbF{slw}")
                    p.need("tensor", f"wsbF{slw}.rdy")
                    p.wait("tensor", "S", stiT)
                    for q4 in range(4):
                        dc = qd * 4 + q4
                        mm(hb.ap(), siTv[dc], wsb[slw][:, q4 * 512:(q4 + 1) * 512],
                           dc == 0, dc == 15, inc=(q4 == 3))
                    p.set_tok(f"wsbF{slw}.free", T=p.c("T"))
                th1 = p.c("T")
                p.set_tok("si.free", T=th1)
                p.wait("vector", "T", th1)
                p.wait("vector", "V", vbr1)
                p.need("vector", f"H{e_}.free")
                vhp = vt_op(lambda e, e_=e_, hb=hb: e.scalar_tensor_tensor(
                    out=Hv[e_], in0=hb.ap(), scalar=1.0, in1=b1rv,
                    op0=ALU.mult, op1=ALU.add))
                p.set_tok(bnm(hb) + ".free", V=vhp)
                p.set_tok("b1rep.free", V=vhp)
                p.wait("scalar", "V", vhp)
                sh = sc_act(Hv[e_], Hv[e_], AF.Gelu)
                p.set_tok(f"H{e_}.rdy", S=sh)
            p.barrier()
            if phases < 6:
                continue

            for e_ in EXPS:
                b = PB[6]
                p.need("tensor", "pb6.free", f"H{e_}.rdy")
                for q4 in range(4):
                    mm(b.ap()[:, q4 * 128:(q4 + 1) * 128],
                       Hv[e_][:, q4 * 128:(q4 + 1) * 128], sm_id, True, True, inc=(q4 == 3))
                tht = p.c("T")
                p.set_tok(f"H{e_}.free", T=tht)
                p.wait("scalar", "T", tht)
                p.need("scalar", "hT.free")
                shT = 0
                for q4 in range(4):
                    shT = sc_act(hTv[q4], b.ap()[:, q4 * 128:(q4 + 1) * 128], AF.Copy,
                                 inc=(q4 == 3))
                p.set_tok("pb6.free", S=shT)
                b2q = dma_in(row_a, d_b2[e_:e_ + 1, :], "rowa")
                p.wait("vector", "D_rowa", b2q)
                p.need("vector", "rowbb.free")
                vb2 = vt_op(lambda e: e.tensor_copy(out=row_bb, in_=row_a))
                p.set_tok("rowa.free", V=vb2)
                p.need("tensor", "pb0.free", "pb1.free", "pb2.free", "pb3.free")
                for hc in range(4):
                    slw = hc % 2
                    stream2(d_w2[e_, hc * 128:(hc + 1) * 128, :], wsb[slw], f"wsbF{slw}")
                    p.need("tensor", f"wsbF{slw}.rdy")
                    if hc == 0:
                        p.wait("tensor", "S", shT)
                    for dg in range(4):
                        mm(PB[dg].ap(), hTv[hc], wsb[slw][:, dg * 512:(dg + 1) * 512],
                           hc == 0, False, inc=(dg == 3))
                    p.set_tok(f"wsbF{slw}.free", T=p.c("T"))
                p.wait("tensor", "V", vb2)
                for dg in range(4):
                    mm(PB[dg].ap(), sm_onrb, row_bb[:, dg * 512:(dg + 1) * 512],
                       False, True, inc=(dg == 3))
                tso = p.c("T")
                p.set_tok("rowbb.free", T=tso)
                p.set_tok("hT.free", T=tso)
                p.wait("scalar", "T", tso)
                p.need("scalar", f"so{e_}.free")
                sso = 0
                for dg in range(4):
                    sso = sc_act(soutv[e_][:, dg * 512:(dg + 1) * 512], PB[dg].ap(), AF.Copy,
                                 inc=(dg == 3))
                for dg in range(4):
                    p.set_tok(f"pb{dg}.free", S=sso)
                p.set_tok(f"so{e_}.rdy", S=sso)
            p.barrier()
            if phases < 7:
                continue

            for n_ in range(NCH):
                grp = [PB[4 * (n_ % 2) + i] for i in range(4)]
                nx = dma_in(xrc, d_x1sp[n_ * 128:(n_ + 1) * 128, :], "r7")
                for dg in range(4):
                    b = grp[dg]
                    p.need("tensor", bnm(b) + ".free")
                    if n_ == 0 and dg == 0:
                        for e2 in EXPS:
                            p.need("tensor", f"so{e2}.rdy")
                    for esc in range(NE):
                        mm(b.ap(), eTv[esc][:, n_ * 128:(n_ + 1) * 128],
                           soutv[esc][:, dg * 512:(dg + 1) * 512],
                           esc == 0, esc == NE - 1, inc=(dg == 3 and esc == NE - 1))
                tcb = p.c("T")
                p.wait("vector", "T", tcb)
                p.need("vector", "r7.rdy")
                for dg in range(4):
                    vy2 = vt_op(lambda e, dg=dg, grp=grp, n_=n_: e.scalar_tensor_tensor(
                        out=xrc[:, dg * 512:(dg + 1) * 512], in0=grp[dg].ap(),
                        scalar=sm_rsi[:, n_:n_ + 1],
                        in1=xrc[:, dg * 512:(dg + 1) * 512], op0=ALU.mult, op1=ALU.add,
                        accum_out=sm_sc1[:, dg:dg + 1]), inc=(dg == 3))
                for dg in range(4):
                    p.set_tok(bnm(grp[dg]) + ".free", V=vy2)
                ch("vector")
                vs2 = vt_op(lambda e: e.tensor_reduce(out=sm_st[:, 0:1], in_=sm_sc1[:, 0:4],
                                                      axis=AX.X, op=ALU.add))
                p.wait("scalar", "V", vs2)
                rstd, bcol = stats_ln(xrc, 0)
                sout_ = sc_act(xrc, xrc, AF.Identity, bias=bcol, scale=rstd)
                ch("scalar")
                nsp = dma_out(d_out[el, n_ * 128:(n_ + 1) * 128, :], xrc, "D_str7")
                p.set_tok("r7.free", D_str7=nsp)

        for semname, cnt in sorted(p.counts.items()):
            if semname.startswith("D_") and cnt > 0:
                eng = "scalar" if semname in ("D_str7", "D_stxb0", "D_stxb1") else "sync"
                if semname == "D_dbg":
                    eng = "gpsimd"
                p.wait(eng, semname, cnt)

        sems = {s: E(nc.semaphore(s)) for s in sorted(p.counts)}

        block = E(nc.Block())

        @block.sync
        def _(e):
            p.replay("sync", e, sems)

        @block.tensor
        def _(e):
            p.replay("tensor", e, sems)

        @block.scalar
        def _(e):
            p.replay("scalar", e, sems)

        @block.vector
        def _(e):
            p.replay("vector", e, sems)

        @block.gpsimd
        def _(e):
            p.replay("gpsimd", e, sems)

    return nc


def _stage_weights(inputs):
    wq = np.asarray(inputs["L0_attn_wq"], np.float32)
    wq_r = np.ascontiguousarray(
        wq.reshape(16, 128, 16, 128).transpose(2, 1, 0, 3).reshape(16, 128, 2048))
    mu = np.asarray(inputs["L0_ffn_mu"], np.float32)
    mu_r = np.ascontiguousarray(
        mu.reshape(16, 128, 16, 128).transpose(2, 1, 0, 3).reshape(16, 128, 2048))
    w = dict(
        wq_r=wq_r,
        wk=np.ascontiguousarray(np.asarray(inputs["L0_attn_wk"], np.float32)),
        wv=np.ascontiguousarray(np.asarray(inputs["L0_attn_wv"], np.float32)),
        wo=np.ascontiguousarray(np.asarray(inputs["L0_attn_wo"], np.float32)),
        mu_r=mu_r,
        W1=np.ascontiguousarray(np.asarray(inputs["L0_ffn_W1"], np.float32)),
        W2=np.ascontiguousarray(np.asarray(inputs["L0_ffn_W2"], np.float32)),
        b1=np.ascontiguousarray(np.asarray(inputs["L0_ffn_b1"], np.float32)),
        b2=np.ascontiguousarray(np.asarray(inputs["L0_ffn_b2"], np.float32)),
        gamma=np.ascontiguousarray(np.asarray(inputs["L0_ffn_gamma"], np.float32).reshape(1, C)),
        beta=np.ascontiguousarray(np.asarray(inputs["L0_ffn_beta"], np.float32).reshape(1, C)),
        scale=np.ascontiguousarray(np.asarray(inputs["L0_ffn_scale"], np.float32).reshape(1, 1)),
    )
    w.update(host_constants())
    return w


def kernel(**inputs):
    x = np.asarray(inputs["L0_attn_x"], np.float32)
    w = _stage_weights(inputs)
    nc = build_program()
    in_maps = []
    for core in range(8):
        xs = np.ascontiguousarray(x[2 * core:2 * core + 2])
        xTs = np.ascontiguousarray(xs.transpose(0, 2, 1))
        m = dict(x=xs, xT=xTs)
        m.update(w)
        in_maps.append(m)
    res = run_bass_kernel_spmd(nc, in_maps, core_ids=list(range(8)))
    out = np.concatenate([np.asarray(r["out"]) for r in res.results], axis=0)
    return out


# version 05
# speedup vs baseline: 1.0450x; 1.0450x over previous
import numpy as np
import ml_dtypes
import contextlib

import concourse.bass as bass
import concourse.mybir as mybir
from concourse.bass_utils import run_bass_kernel_spmd

F32 = mybir.dt.float32
BF16 = mybir.dt.bfloat16
F16 = mybir.dt.float16
U8 = mybir.dt.uint8
AF = mybir.ActivationFunctionType
ALU = mybir.AluOpType
AX = mybir.AxisListType
BF = ml_dtypes.bfloat16

T = 2048
C = 2048
NH = 16
NKV = 4
HD = 128
NE = 16
HH = 512
NCH = 16
NELEM = 2
LN_EPS = 1e-5
L2_EPS = 1e-12
KB = 1024
PROG_SEMS = ["T", "S", "V", "Pq"]
OWN = {"tensor": "T", "scalar": "S", "vector": "V", "gpsimd": "Pq", "sync": None}


def host_constants():
    d = HD
    inv_freq = 1.0 / (10000.0 ** (np.arange(0, d, 2, dtype=np.float64) / d))
    t = np.arange(T, dtype=np.float64)
    freqs = t[:, None] * inv_freq[None, :]
    emb = np.concatenate([freqs, freqs], axis=-1)
    s4 = float(d) ** -0.25
    cs = np.stack([np.asarray((np.cos(emb).T * s4).astype(np.float16)),
                   np.asarray((np.sin(emb).T * s4).astype(np.float16))])
    tri = (np.arange(128)[:, None] <= np.arange(128)[None, :]).astype(BF)
    return dict(
        cs=np.ascontiguousarray(cs),
        tri=np.ascontiguousarray(tri),
        ident=np.eye(128, dtype=BF),
        ones_c=np.ones((128, 1), dtype=BF),
        twos_c=np.full((128, 1), 2.0, dtype=BF),
        ones_rb=np.ones((1, 128), dtype=BF),
        ones_rf=np.ones((1, 128), dtype=np.float32),
        one_f=np.ones((1, 1), dtype=np.float32),
    )


class B:
    def __init__(self):
        self.events = []
        self.counts = {s: 0 for s in PROG_SEMS}
        self.tok = {}

    def op(self, eng, fn, inc=()):
        for s, a in inc:
            self.counts[s] = self.counts.get(s, 0) + a
        self.events.append((eng, "op", fn, tuple(inc)))

    def wait(self, eng, sem, val):
        if val and val > 0:
            self.events.append((eng, "wait", sem, val))

    def c(self, sem):
        return self.counts.get(sem, 0)

    def mark(self, name, *semlist):
        d = self.tok.setdefault(name, {})
        for s in semlist:
            d[s] = max(d.get(s, 0), self.c(s))

    def set_tok(self, name, **kw):
        self.tok[name] = dict(kw)

    def need(self, eng, *names):
        for name in names:
            for s, n in self.tok.get(name, {}).items():
                self.wait(eng, s, n)

    def barrier(self):
        bar = {s: self.c(s) for s in self.counts}
        for eng in ("sync", "tensor", "scalar", "vector", "gpsimd"):
            own = OWN[eng]
            for s, n in bar.items():
                if s != own and n > 0:
                    self.wait(eng, s, n)
        for name in list(self.tok):
            d = {s: n for s, n in self.tok[name].items() if n > bar[s]}
            if d:
                self.tok[name] = d
            else:
                del self.tok[name]

    def replay(self, eng_name, eng, sems):
        for ev in self.events:
            if ev[0] != eng_name:
                continue
            if ev[1] == "op":
                inst = ev[2](eng)
                for s, a in ev[3]:
                    inst.then_inc(sems[s], a)
            else:
                eng.wait_ge(sems[ev[2]], ev[3])


def build_program(lite=False, debug=False, phases=7, nelem=NELEM, a0stop=9, ropemode=2):
    nc = bass.Bass("TRN2", target_bir_lowering=False)

    d_x = nc.dram_tensor("x", [NELEM, T, C], F32, kind="ExternalInput").ap()
    d_xT = nc.dram_tensor("xT", [NELEM, C, T], F32, kind="ExternalInput").ap()
    d_wq = nc.dram_tensor("wq_r", [NH, 128, C], F32, kind="ExternalInput").ap()
    d_wk = nc.dram_tensor("wk", [C, NKV * HD], F32, kind="ExternalInput").ap()
    d_wv = nc.dram_tensor("wv", [C, NKV * HD], F32, kind="ExternalInput").ap()
    d_wo = nc.dram_tensor("wo", [C, C], F32, kind="ExternalInput").ap()
    d_mu = nc.dram_tensor("mu_r", [NE, 128, C], F32, kind="ExternalInput").ap()
    d_w1 = nc.dram_tensor("W1", [NE, C, HH], F32, kind="ExternalInput").ap()
    d_w2 = nc.dram_tensor("W2", [NE, HH, C], F32, kind="ExternalInput").ap()
    d_b1 = nc.dram_tensor("b1", [NE, HH], F32, kind="ExternalInput").ap()
    d_b2 = nc.dram_tensor("b2", [NE, C], F32, kind="ExternalInput").ap()
    d_gam = nc.dram_tensor("gamma", [1, C], F32, kind="ExternalInput").ap()
    d_bet = nc.dram_tensor("beta", [1, C], F32, kind="ExternalInput").ap()
    d_scale = nc.dram_tensor("scale", [1, 1], F32, kind="ExternalInput").ap()
    d_cs = nc.dram_tensor("cs", [2, 128, T], F16, kind="ExternalInput").ap()
    d_tri = nc.dram_tensor("tri", [128, 128], BF16, kind="ExternalInput").ap()
    d_id = nc.dram_tensor("ident", [128, 128], BF16, kind="ExternalInput").ap()
    d_onc = nc.dram_tensor("ones_c", [128, 1], BF16, kind="ExternalInput").ap()
    d_two = nc.dram_tensor("twos_c", [128, 1], BF16, kind="ExternalInput").ap()
    d_onrb = nc.dram_tensor("ones_rb", [1, 128], BF16, kind="ExternalInput").ap()
    d_onrf = nc.dram_tensor("ones_rf", [1, 128], F32, kind="ExternalInput").ap()
    d_onef = nc.dram_tensor("one_f", [1, 1], F32, kind="ExternalInput").ap()

    d_out = nc.dram_tensor("out", [NELEM, T, C], F32, kind="ExternalOutput").ap()
    spkind = "ExternalOutput" if debug else "Internal"
    d_x1sp = nc.dram_tensor("x1sp", [T, C], F32, kind=spkind).ap()
    d_xlnsp = nc.dram_tensor("xlnsp", [T, C], BF16, kind=spkind).ap()
    d_wbc = nc.dram_tensor("wbc", [200, 128, C], BF16, kind="Internal").ap()
    if debug:
        d_dbgb = nc.dram_tensor("dbgb", [20, 128, 2048], BF16, kind="ExternalOutput").ap()
    dbg_i = [0]

    HEADS = list(range(2 if lite else NH))
    EXPS = list(range(2 if lite else NE))

    st = contextlib.ExitStack()
    with st:
        E = st.enter_context
        TOTAL = 212800
        AR = E(nc.sbuf_tensor("AR", [128, TOTAL], U8)).ap()

        def v(off, dtype, cols, rows=128):
            esz = {F32: 4, BF16: 2, F16: 2}[dtype]
            assert off + cols * esz <= TOTAL, (off, cols)
            ap = AR[:, off:off + cols * esz].bitcast(dtype)
            if rows != 128:
                ap = ap[0:rows, :]
            return ap

        R0 = 0
        R1 = R0 + 65536
        R2 = R1 + 65536
        R3 = R2 + 32768
        R4 = R3 + 12288
        R5 = R4 + 8192
        R6 = R5 + 8192
        R7 = R6 + 5696
        R8 = R7 + 8192
        SM = R8 + 4096
        assert SM + 2304 <= TOTAL

        sm_tri = v(SM + 0, BF16, 128)
        sm_id = v(SM + 256, BF16, 128)
        sm_onc = v(SM + 512, BF16, 1)
        sm_two = v(SM + 576, BF16, 1)
        sm_onrb = v(SM + 640, BF16, 128, rows=1)
        sm_onrf = v(SM + 896, F32, 128, rows=1)
        sm_onef = v(SM + 1408, F32, 1, rows=1)
        sm_negS = v(SM + 1472, F32, 1)
        sm_scol = v(SM + 1536, F32, 1)
        sm_r = v(SM + 1600, F32, 16)
        sm_csi = v(SM + 1664, F32, 16)
        sm_rsi = v(SM + 1728, F32, 16)
        sm_csc = v(SM + 1792, F32, 16)
        sm_st = v(SM + 1856, F32, 16)
        sm_sc1 = v(SM + 1920, F32, 16)
        sm_csp = v(SM + 1984, F32, 64)

        xTv = [v(R0 + cc * 4 * KB, BF16, 2048) for cc in range(NCH)]
        wob = [v(R0 + cc * 4 * KB, BF16, 2048) for cc in range(NCH)]
        eTv = [v(R0 + i * 4 * KB, BF16, 2048) for i in range(NE)]
        wkb = [v(R1 + cc * KB, BF16, 512) for cc in range(NCH)]
        wkb_slab = [v(R1 + q * 4 * KB, BF16, 2048) for q in range(4)]
        wvb = [v(R1 + 16 * KB + cc * KB, BF16, 512) for cc in range(NCH)]
        wvb_slab = [v(R1 + 16 * KB + q * 4 * KB, BF16, 2048) for q in range(4)]
        oTv = v(R1, BF16, 32768)
        xnTr = v(R1, BF16, 16384).rearrange("p (dc n) -> p dc n", n=1024)
        xlnv = [v(R1 + i * 4 * KB, BF16, 2048) for i in range(NCH)]
        soutv = [v(R1 + i * 4 * KB, BF16, 2048) for i in range(NE)]
        kTv = [v(R2 + kv * 4 * KB, BF16, 2048) for kv in range(NKV)]
        vtv = [v(R2 + 16 * KB + tc * KB, BF16, 512) for tc in range(NCH)]
        xlnb = [v(R2 + i * 4 * KB, BF16, 2048) for i in range(2)]
        munb = v(R2 + 16 * KB, BF16, 2048)
        Hv = [v(R2 + e_ * KB, BF16, 512) for e_ in range(NE)]
        Eev = [v(R2 + 16 * KB + n_ * 256, BF16, 128) for n_ in range(NCH)]
        siv = v(R2 + 20 * KB, BF16, 2048)
        siTv = [v(R2 + 24 * KB + dc * 256, BF16, 128) for dc in range(NCH)]
        hTv = [v(R2 + 28 * KB + hc * 256, BF16, 128) for hc in range(4)]
        ebuf = [v(R3 + j * KB, BF16, 512) for j in range(8)]
        b2rv = v(R3, BF16, 2048)
        b1rv = v(R3 + 4 * KB, BF16, 512)
        rrow = [v(R3 + i * 2 * KB, F32, 512, rows=1) for i in range(2)]
        gamrep = v(R3, BF16, 2048)
        betrep = v(R3 + 4 * KB, BF16, 2048)
        row_a = v(R3, F32, 2048, rows=1)
        row_bf = v(R3, F32, 512, rows=1)
        row_bb = v(R3 + 8 * KB, BF16, 2048, rows=1)
        wst = [v(R4 + i * 4 * KB, F32, 1024) for i in range(2)]
        wsb = [v(R5 + i * 4 * KB, BF16, 2048) for i in range(2)]
        dscr = [v(R6 + i * 256, BF16, 128) for i in range(2)]
        elscr = [v(R6 + 512 + i * 256, BF16, 128) for i in range(4)]
        rope_rot = v(R6, BF16, 512)
        olbv = v(R6 + 1536, BF16, 512)
        rope_t1 = v(R6 + 2560, BF16, 512)
        rope_t2 = v(R6 + 3584, BF16, 512)
        ogbv = v(R6 + 4608, BF16, 512)
        musq = v(R6, BF16, 2048)
        repb = v(R6 + 4096, BF16, 128)
        irow = v(R6 + 4352, F32, 128, rows=1)
        irow2 = v(R6 + 4864, F32, 128, rows=1)
        dumpb = v(R6, BF16, 2048)
        cosv = v(R7, F16, 2048)
        sinv = v(R7 + 4 * KB, F16, 2048)
        xrc = v(R7, F32, 2048)
        qsv = v(R8, BF16, 2048)
        xnc = v(R8, BF16, 2048)

        PB = [E(nc.psum_tensor(f"pb{i}", [128, 512], F32)) for i in range(8)]
        PBN = {id(PB[i]): f"pb{i}" for i in range(8)}

        p = B()
        rr = [0]
        wci = [0]
        cmode = [0]
        WCSEM = {"wsbF0": "D_wbcr0", "wsbF1": "D_wbcr1", "wsbq": "D_wbcrq"}
        WWSEM = {"wsbF0": "D_wbw0", "wsbF1": "D_wbw1", "wsbq": "D_wbwq"}

        def dma_in(dst, src, bufname, eng="sync", sem=None):
            sem = sem or ("D_" + bufname)
            p.need(eng, bufname + ".free", bufname + ".wc")
            p.op(eng, lambda e, d=dst, s=src: e.dma_start(out=d, in_=s),
                 inc=[(sem, 16)])
            p.set_tok(bufname + ".rdy", **{sem: p.c(sem)})
            return p.c(sem)

        def dma_out(dst, src, sem):
            p.op("scalar", lambda e, d=dst, s=src: e.dma_start(out=d, in_=s),
                 inc=[(sem, 16)])
            return p.c(sem)

        def stream2(dram_ap, dst_bf, nm, cache=True):
            if cache and cmode[0] == 1:
                sidx = wci[0]
                wci[0] += 1
                rsem = WCSEM.get(nm) or (
                    "D_wbcrw" if nm.startswith("wob") else "D_wbcrk")
                p.need("sync", nm + ".free")
                p.op("sync", lambda e, d=dst_bf, s=d_wbc[sidx]: e.dma_start(out=d, in_=s),
                     inc=[(rsem, 16)])
                tk = {rsem: p.c(rsem)}
                p.set_tok(nm + ".rdy", **tk)
                return tk
            sidx = wci[0] if cache else None
            if cache:
                wci[0] += 1
            three_d = len(dram_ap.shape) == 3
            toks = {}
            for hf in range(2):
                sl = rr[0] % 2
                ceng = "gpsimd" if sl == 0 else "vector"
                csem = OWN[ceng]
                rr[0] += 1
                if three_d:
                    src = dram_ap[:, 2 * hf:2 * hf + 2, :]
                    dst_st = wst[sl].rearrange("p (a n) -> p a n", n=dram_ap.shape[2])
                else:
                    src = dram_ap[:, hf * 1024:(hf + 1) * 1024]
                    dst_st = wst[sl]
                dma_in(dst_st, src, f"wst{sl}")
                p.need(ceng, f"wst{sl}.rdy", nm + ".free", nm + ".wc")
                p.op(ceng, lambda e, d=dst_bf[:, hf * 1024:(hf + 1) * 1024], s=wst[sl]:
                     e.tensor_copy(out=d, in_=s), inc=[(csem, 1)])
                p.mark(f"wst{sl}.free", csem)
                toks[csem] = p.c(csem)
                if cache:
                    wsem = WWSEM.get(nm, "D_wbws")
                    p.wait("gpsimd", csem, p.c(csem))
                    p.op("gpsimd", lambda e, d=d_wbc[sidx][:, hf * 1024:(hf + 1) * 1024],
                         s=dst_bf[:, hf * 1024:(hf + 1) * 1024]: e.dma_start(out=d, in_=s),
                         inc=[(wsem, 16)])
            if cache:
                wsem = WWSEM.get(nm, "D_wbws")
                p.set_tok(nm + ".wc", **{wsem: p.c(wsem)})
            p.set_tok(nm + ".rdy", **toks)
            return toks

        def mm(out_ap, lhsT, rhs, start, stop, inc=False):
            p.op("tensor", lambda e, o=out_ap, l=lhsT, r=rhs, s=start, t=stop:
                 e.matmul(out=o, lhsT=l, rhs=r, start=s, stop=t),
                 inc=[("T", 1)] if inc else ())
            return p.c("T")

        def bnm(b):
            return PBN[id(b)]

        def sc_act(out, in_, func, bias=0.0, scale=1.0, accum=None, inc=True):
            p.op("scalar", lambda e, o=out, i=in_, f=func, bi=bias, sc=scale, a=accum:
                 (e.activation(out=o, in_=i, func=f, bias=bi, scale=sc, accum_out=a)
                  if a is not None else
                  e.activation(out=o, in_=i, func=f, bias=bi, scale=sc)),
                 inc=[("S", 1)] if inc else ())
            return p.c("S")

        def vt_op(fn, inc=True):
            p.op("vector", fn, inc=[("V", 1)] if inc else ())
            return p.c("V")

        def ch(eng):
            p.wait(eng, OWN[eng], p.c(OWN[eng]))

        def dbg(src, cols=2048):
            if not debug:
                return
            i = dbg_i[0]
            dbg_i[0] += 1
            if i >= 20:
                return
            p.op("gpsimd", lambda e, s=src, i=i, c=cols: e.dma_start(
                out=d_dbgb[i][:, 0:c], in_=s), inc=[("D_dbg", 16)])

        def rope_evac(b, dst, tg, tmm):
            bap = b.ap()
            cseg = cosv[:, tg * 512:(tg + 1) * 512]
            sseg = sinv[:, tg * 512:(tg + 1) * 512]
            if ropemode == 0:
                p.wait("scalar", "T", tmm)
                s0 = sc_act(dst, bap, AF.Copy)
                p.set_tok(bnm(b) + ".free", S=s0)
                return
            if ropemode == 1:
                p.wait("scalar", "T", tmm)
                p.need("scalar", "ropescr.free")
                p.op("scalar", lambda e: e.activation(out=rope_rot[0:64, :], in_=bap[64:128, :],
                                                      func=AF.Copy, bias=0.0, scale=-1.0))
                p.op("scalar", lambda e: e.copy(out=rope_rot[64:128, :], in_=bap[0:64, :]))
                s0 = sc_act(dst, bap, AF.Copy)
                p.set_tok("ropescr.free", S=s0)
                p.set_tok(bnm(b) + ".free", S=s0)
                return
            p.wait("scalar", "T", tmm)
            p.need("scalar", "ropescr.free")
            p.op("scalar", lambda e: e.activation(out=rope_rot[0:64, :], in_=bap[64:128, :],
                                                  func=AF.Copy, bias=0.0, scale=-1.0))
            p.op("scalar", lambda e: e.copy(out=rope_rot[64:128, :], in_=bap[0:64, :]),
                 inc=[("S", 1)])
            srot = p.c("S")
            p.wait("vector", "T", tmm)
            p.need("vector", "ropescr.free")
            p.wait("vector", "S", srot)
            vt_op(lambda e: e.tensor_tensor(out=rope_t1, in0=bap, in1=cseg, op=ALU.mult), inc=False)
            vt_op(lambda e: e.tensor_tensor(out=rope_t2, in0=rope_rot, in1=sseg, op=ALU.mult))
            ch("vector")
            vt_op(lambda e, d=dst: e.tensor_tensor(out=d, in0=rope_t1, in1=rope_t2, op=ALU.add))
            p.set_tok("ropescr.free", V=p.c("V"))
            p.set_tok(bnm(b) + ".free", V=p.c("V"), S=srot)

        def stats_ln(ybuf, sum_col):
            i = sum_col
            p.need("scalar", "dumpb.free")
            ssq = sc_act(dumpb, ybuf, AF.Square, accum=sm_st[:, i + 1:i + 2])
            p.set_tok("dumpb.free", S=ssq)
            p.wait("vector", "S", ssq)
            vt_op(lambda e: e.tensor_scalar(out=sm_st[:, i + 2:i + 3], in0=sm_st[:, i:i + 1],
                                            scalar1=1.0 / 2048, scalar2=None, op0=ALU.mult))
            vt_op(lambda e: e.tensor_scalar(out=sm_st[:, i + 4:i + 5], in0=sm_st[:, i + 1:i + 2],
                                            scalar1=1.0 / 2048, scalar2=None, op0=ALU.mult))
            ch("vector")
            vt_op(lambda e: e.tensor_tensor(out=sm_st[:, i + 3:i + 4], in0=sm_st[:, i + 2:i + 3],
                                            in1=sm_st[:, i + 2:i + 3], op=ALU.mult))
            ch("vector")
            vt_op(lambda e: e.tensor_tensor(out=sm_st[:, i + 5:i + 6],
                                            in0=sm_st[:, i + 4:i + 5],
                                            in1=sm_st[:, i + 3:i + 4], op=ALU.subtract))
            ch("vector")
            vvar = vt_op(lambda e: e.tensor_scalar(out=sm_st[:, i + 5:i + 6],
                                                   in0=sm_st[:, i + 5:i + 6],
                                                   scalar1=LN_EPS, scalar2=None, op0=ALU.add))
            p.wait("scalar", "V", vvar)
            sstd = sc_act(sm_st[:, i + 6:i + 7], sm_st[:, i + 5:i + 6], AF.Sqrt)
            p.wait("vector", "S", sstd)
            vt_op(lambda e: e.reciprocal(out=sm_st[:, i + 7:i + 8],
                                         in_=sm_st[:, i + 6:i + 7]))
            ch("vector")
            vstat = vt_op(lambda e: e.tensor_scalar(out=sm_st[:, i + 8:i + 9],
                                                    in0=sm_st[:, i + 2:i + 3],
                                                    scalar1=sm_st[:, i + 7:i + 8],
                                                    scalar2=-1.0, op0=ALU.mult, op1=ALU.mult))
            p.wait("scalar", "V", vstat)
            return sm_st[:, i + 7:i + 8], sm_st[:, i + 8:i + 9]

        for dst, src in ((sm_tri, d_tri), (sm_id, d_id), (sm_onc, d_onc),
                         (sm_two, d_two), (sm_onrb, d_onrb), (sm_onrf, d_onrf),
                         (sm_onef, d_onef)):
            p.op("sync", lambda e, d=dst, s=src: e.dma_start(out=d, in_=s), inc=[("D_const", 16)])
        p.op("sync", lambda e: e.dma_start(out=row_bf[:, 0:1], in_=d_scale), inc=[("D_const", 16)])
        cq = p.c("D_const")
        p.wait("tensor", "D_const", cq)
        mm(PB[7].ap()[:, 0:1], sm_onrf, row_bf[:, 0:1], True, True, inc=True)
        p.wait("vector", "T", p.c("T"))
        vt_op(lambda e: e.tensor_copy(out=sm_scol, in_=PB[7].ap()[:, 0:1]))
        ch("vector")
        vt_op(lambda e: e.tensor_scalar(out=sm_st[:, 0:1], in0=sm_scol,
                                        scalar1=-1.0, scalar2=None, op0=ALU.mult))
        ch("vector")
        vns = vt_op(lambda e: e.tensor_tensor(out=sm_negS, in0=sm_scol,
                                              in1=sm_st[:, 0:1], op=ALU.min))
        p.set_tok("pb7.free", V=vns)
        p.mark("consts.rdy", "V", "D_const")

        if lite:
            p.op("gpsimd", lambda e: e.memset(oTv, 0.0))
            p.op("gpsimd", lambda e: e.memset(v(R2, BF16, 8192), 0.0))
            p.op("gpsimd", lambda e: e.memset(v(R2 + 16 * KB, BF16, 8192), 0.0),
                 inc=[("Pq", 1)])
            p.mark("litezero.rdy", "Pq")

        for el in range(nelem):
            if el > 0:
                p.barrier()
            cmode[0] = 1 if el > 0 else 0
            wci[0] = 0

            if a0stop < 1:
                continue
            dma_in(cosv, d_cs[0], "cosv", sem="D_cs")
            dma_in(sinv, d_cs[1], "sinv", sem="D_cs")
            cstok = p.c("D_cs")
            for cc in range(NCH):
                stream2(d_xT[el, cc * 128:(cc + 1) * 128, :], xTv[cc], f"xT{cc}",
                        cache=False)
            if a0stop < 2:
                continue
            allt = {}
            for nm, dsrc, slabs in (("wk", d_wk, wkb_slab), ("wv", d_wv, wvb_slab)):
                tks = {}
                for q in range(4):
                    ap = dsrc[q * 512:(q + 1) * 512, :].rearrange("(a p) n -> p a n", p=128)
                    tks.update(stream2(ap, slabs[q], f"{nm}{q}"))
                allt.update(tks)
                p.set_tok(nm + ".rdy", **tks)
            if cmode[0] == 1:
                p.set_tok("wk.rdy", **allt)

            if a0stop < 3:
                continue
            import os
            KT = int(os.environ.get("KTILES", 16))
            for kv in range(NKV):
                for tg in range(4):
                    if kv * 4 + tg >= KT:
                        continue
                    b = PB[(kv * 4 + tg) % 2]
                    p.need("tensor", bnm(b) + ".free", "wk.rdy")
                    if kv == 0 and tg == 0:
                        p.wait("tensor", "D_cs", cstok)
                        for cc in range(NCH):
                            p.need("tensor", f"xT{cc}.rdy")
                    for cc in range(NCH):
                        mm(b.ap(), wkb[cc][:, kv * 128:(kv + 1) * 128],
                           xTv[cc][:, tg * 512:(tg + 1) * 512],
                           cc == 0, cc == NCH - 1, inc=(cc == NCH - 1))
                    rope_evac(b, kTv[kv][:, tg * 512:(tg + 1) * 512], tg, p.c("T"))
            if a0stop < 4:
                continue
            for tc in range(NCH):
                b = PB[4 + tc % 2]
                p.need("tensor", bnm(b) + ".free", "wv.rdy")
                for cc in range(NCH):
                    mm(b.ap(), xTv[cc][:, tc * 128:(tc + 1) * 128], wvb[cc],
                       cc == 0, cc == NCH - 1, inc=(cc == NCH - 1))
                p.wait("scalar", "T", p.c("T"))
                sv = sc_act(vtv[tc], b.ap(), AF.Copy)
                p.set_tok(bnm(b) + ".free", S=sv)
            p.mark("vt.rdy", "S")

            if cmode[0] == 0 and p.c("D_wbws"):
                p.wait("gpsimd", "D_wbws", p.c("D_wbws"))
                p.op("gpsimd", lambda e: e.engine_nop(), inc=[("Pq", 1)])
                p.set_tok("wkvwc.done", Pq=p.c("Pq"))
            p.need("vector", "wkvwc.done")
            for h in (HEADS if phases >= 2 else []):
                kv = h // 4
                stream2(d_wq[h], wsb[0], "wsbq")
                for tg in range(4):
                    b = PB[6]
                    p.need("tensor", "pb6.free", "wsbq.rdy", "qs.free")
                    for cc in range(NCH):
                        mm(b.ap(), wsb[0][:, cc * 128:(cc + 1) * 128],
                           xTv[cc][:, tg * 512:(tg + 1) * 512],
                           cc == 0, cc == NCH - 1, inc=(cc == NCH - 1))
                    tmm = p.c("T")
                    p.need("scalar", "qs.free")
                    p.need("vector", "qs.free")
                    rope_evac(b, qsv[:, tg * 512:(tg + 1) * 512], tg, tmm)
                    p.set_tok("pb6.free", **p.tok[bnm(b) + ".free"])
                p.set_tok("qs.rdy", V=p.c("V"), S=p.c("S"))
                p.set_tok("ropescr.used", V=p.c("V"), S=p.c("S"))
                p.tok.pop("qs.free", None)
                p.mark("wsbq.free", "T")

                for g in range(4):
                    jmax = 4 * g + 3
                    etok = {}
                    p.need("tensor", "pb2.free", "pb4.free", "vt.rdy")

                    def emit_pv(j, g=g, kv=kv):
                        pp = j - 4 * g
                        c0 = max(0, pp) * 128
                        stk, vtk = etok[j]
                        p.wait("tensor", "S", stk)
                        if vtk:
                            p.wait("tensor", "V", vtk)
                        mm(PB[2].ap()[:, c0:512], vtv[j][:, kv * 128:(kv + 1) * 128],
                           ebuf[j % 8][:, c0:512], j == 0, j == jmax)
                        mm(PB[4].ap()[0:1, c0:512], sm_two, ebuf[j % 8][:, c0:512],
                           j == 0, j == jmax, inc=True)
                        p.set_tok(f"eb{j % 8}.free", T=p.c("T"))

                    for j in range(jmax + 1):
                        pp = j - 4 * g
                        c0 = max(0, pp) * 128
                        b = PB[j % 2]
                        p.need("tensor", bnm(b) + ".free")
                        if j == 0:
                            p.need("tensor", "qs.rdy")
                        mm(b.ap()[:, c0:512], kTv[kv][:, j * 128:(j + 1) * 128],
                           qsv[:, g * 512 + c0:(g + 1) * 512], True, True, inc=True)
                        tmm = p.c("T")
                        if j >= 1:
                            emit_pv(j - 1)
                        p.wait("scalar", "T", tmm)
                        p.need("scalar", f"eb{j % 8}.free")
                        vtk = 0
                        if pp >= 0:
                            ds = dscr[j % 2]
                            p.need("scalar", f"ds{j % 2}.free", "ropescr.used")
                            sdiag = sc_act(ds, b.ap()[:, c0:c0 + 128], AF.Exp)
                            p.wait("vector", "S", sdiag)
                            p.need("vector", f"eb{j % 8}.free")
                            vtk = vt_op(lambda e, j=j, c0=c0, ds=ds: e.tensor_tensor(
                                out=ebuf[j % 8][:, c0:c0 + 128], in0=ds, in1=sm_tri,
                                op=ALU.mult))
                            p.set_tok(f"ds{j % 2}.free", V=vtk)
                            if c0 + 128 < 512:
                                sc_act(ebuf[j % 8][:, c0 + 128:512], b.ap()[:, c0 + 128:512],
                                       AF.Exp)
                        else:
                            sc_act(ebuf[j % 8], b.ap(), AF.Exp)
                        stk = p.c("S")
                        p.set_tok(bnm(b) + ".free",
                                  **({"S": stk, "V": vtk} if vtk else {"S": stk}))
                        etok[j] = (stk, vtk)
                    emit_pv(jmax)
                    eltile = {}
                    for isub in range(4):
                        i = 4 * g + isub
                        if i >= 2:
                            stk, _ = etok[i - 1]
                            p.wait("vector", "S", stk)
                            p.need("vector", f"el{isub}.free", "ropescr.used")
                            vtk = vt_op(lambda e, i=i, isub=isub: e.tensor_tensor(
                                out=elscr[isub],
                                in0=ebuf[(i - 1) % 8][:, isub * 128:(isub + 1) * 128],
                                in1=sm_tri, op=ALU.mult))
                            eltile[i] = (elscr[isub], vtk)
                    p.need("tensor", "pb3.free", "pb5.free")
                    for isub in range(4):
                        i = 4 * g + isub
                        cc0 = isub * 128
                        if i == 0:
                            parts = [(0, ebuf[0][:, 0:128])]
                        elif i == 1:
                            parts = [(0, ebuf[0][:, 128:256]), (1, ebuf[1][:, 128:256])]
                        else:
                            slr, vtk = eltile[i]
                            p.wait("tensor", "V", vtk)
                            if i % 2 == 0:
                                parts = [(i - 1, slr)]
                            else:
                                parts = [(i - 2, ebuf[(i - 2) % 8][:, cc0:cc0 + 128]), (i - 1, slr)]
                        for pi, (kchunk, rhs) in enumerate(parts):
                            lastp = pi == len(parts) - 1
                            mm(PB[3].ap()[:, cc0:cc0 + 128],
                               vtv[kchunk][:, kv * 128:(kv + 1) * 128], rhs, pi == 0, lastp)
                            mm(PB[5].ap()[0:1, cc0:cc0 + 128], sm_two, rhs, pi == 0, lastp,
                               inc=(isub == 3 and lastp))
                    tpv = p.c("T")
                    for isub in range(4):
                        if 4 * g + isub >= 2:
                            p.set_tok(f"el{isub}.free", T=tpv)
                    for j in range(max(0, jmax - 5), jmax + 1):
                        p.set_tok(f"eb{j % 8}.free", T=tpv)
                    p.wait("vector", "T", tpv)
                    p.need("vector", "eb0.free", "eb1.free", "eb2.free", "eb3.free")
                    vt_op(lambda e: e.reciprocal(out=rrow[0], in_=PB[4].ap()[0:1, :]), inc=False)
                    vrec = vt_op(lambda e: e.reciprocal(out=rrow[1], in_=PB[5].ap()[0:1, :]))
                    p.wait("scalar", "T", tpv)
                    p.need("scalar", "ogb.free")
                    sc_act(ogbv, PB[2].ap(), AF.Copy, inc=False)
                    sevac = sc_act(olbv, PB[3].ap(), AF.Copy)
                    p.wait("tensor", "S", sevac)
                    p.wait("tensor", "V", vrec)
                    mm(PB[2].ap(), sm_onrf, rrow[0], True, True)
                    trep = mm(PB[3].ap(), sm_onrf, rrow[1], True, True, inc=True)
                    for j in range(4):
                        p.set_tok(f"eb{j}.free", T=trep)
                    p.set_tok("pb4.free", V=vrec)
                    p.set_tok("pb5.free", V=vrec)
                    p.wait("vector", "T", trep)
                    vt_op(lambda e: e.tensor_tensor(out=ogbv, in0=ogbv, in1=PB[2].ap(),
                                                    op=ALU.mult), inc=False)
                    vt_op(lambda e: e.tensor_tensor(out=olbv, in0=olbv, in1=PB[3].ap(),
                                                    op=ALU.mult))
                    ch("vector")
                    vfin = vt_op(lambda e, h=h, g=g: e.tensor_tensor(
                        out=oTv[:, h * 2048 + g * 512:h * 2048 + (g + 1) * 512],
                        in0=ogbv, in1=olbv, op=ALU.add))
                    p.set_tok("pb2.free", V=vfin)
                    p.set_tok("pb3.free", V=vfin)
                    p.set_tok("ogb.free", V=vfin)
                p.set_tok("qs.free", T=p.c("T"))
                p.set_tok("ropescr.free", T=p.c("T"), V=p.c("V"), S=p.c("S"))
                for i2 in range(2):
                    p.mark(f"ds{i2}.free", "S", "V")
                for i2 in range(4):
                    p.mark(f"el{i2}.free", "T")
            p.mark("oT.rdy", "V")
            if phases < 3:
                continue
            if debug and el == 0:
                p.wait("gpsimd", "V", p.c("V"))
                dbg(kTv[0])
                dbg(vtv[0], 512)
                dbg(oTv[:, 0:2048])
            p.barrier()

            for cc in range(NCH):
                stream2(d_wo[cc * 128:(cc + 1) * 128, :], wob[cc], f"wob{cc}")
            for nmi, (dsrc, dst) in enumerate(((d_gam, gamrep), (d_bet, betrep))):
                for hf in range(2):
                    dma_in(wst[hf][0:1, :], dsrc[:, hf * 1024:(hf + 1) * 1024], f"wst{hf}",
                           sem="D_wstr")
                gq = p.c("D_wstr")
                p.wait("tensor", "D_wstr", gq)
                srow = 0
                for sg in range(4):
                    b = PB[sg % 2]
                    p.need("tensor", bnm(b) + ".free")
                    mm(b.ap(), sm_onrf, wst[sg // 2][0:1, (sg % 2) * 512:(sg % 2 + 1) * 512],
                       True, True, inc=True)
                    p.wait("vector", "T", p.c("T"))
                    srow = vt_op(lambda e, b=b, dst=dst, sg=sg: e.tensor_copy(
                        out=dst[:, sg * 512:(sg + 1) * 512], in_=b.ap()))
                    p.set_tok(bnm(b) + ".free", V=srow)
                p.set_tok("wst0.free", T=p.c("T"))
                p.set_tok("wst1.free", T=p.c("T"))
            p.mark("gbrep.rdy", "V")

            for tc in range(NCH):
                grp = [PB[4 * (tc % 2) + i] for i in range(4)]
                nx = dma_in(xrc, d_x[el, tc * 128:(tc + 1) * 128, :], "r7")
                for dg in range(4):
                    b = grp[dg]
                    p.need("tensor", bnm(b) + ".free")
                    if tc == 0 and dg == 0:
                        p.need("tensor", "oT.rdy", f"wob{NCH - 1}.rdy")
                    for cc in range(NCH):
                        mm(b.ap(), oTv[:, cc * 2048 + tc * 128:cc * 2048 + (tc + 1) * 128],
                           wob[cc][:, dg * 512:(dg + 1) * 512],
                           cc == 0, cc == NCH - 1, inc=(dg == 3 and cc == NCH - 1))
                tmm = p.c("T")
                p.wait("vector", "T", tmm)
                p.need("vector", "r7.rdy")
                for dg in range(4):
                    vy1 = vt_op(lambda e, dg=dg, grp=grp: e.scalar_tensor_tensor(
                        out=xrc[:, dg * 512:(dg + 1) * 512], in0=grp[dg].ap(), scalar=1.0,
                        in1=xrc[:, dg * 512:(dg + 1) * 512], op0=ALU.mult, op1=ALU.add,
                        accum_out=sm_sc1[:, dg:dg + 1]), inc=(dg == 3))
                for dg in range(4):
                    p.set_tok(bnm(grp[dg]) + ".free", V=vy1)
                ch("vector")
                vsum = vt_op(lambda e: e.tensor_reduce(out=sm_st[:, 0:1], in_=sm_sc1[:, 0:4],
                                                       axis=AX.X, op=ALU.add))
                p.wait("scalar", "V", vsum)
                rstd, bcol = stats_ln(xrc, 0)
                sx1 = sc_act(xrc, xrc, AF.Identity, bias=bcol, scale=rstd)
                ch("scalar")
                nsp = dma_out(d_x1sp[tc * 128:(tc + 1) * 128, :], xrc, "D_str7")
                p.wait("vector", "S", sx1)
                p.need("vector", "gbrep.rdy", "dumpb.free")
                vg = vt_op(lambda e: e.tensor_tensor(out=dumpb, in0=xrc, in1=gamrep, op=ALU.mult))
                p.set_tok("r7.free", V=vg, D_str7=nsp)
                sl2 = tc % 2
                p.need("vector", f"xlnb{sl2}.free")
                ch("vector")
                vxln = vt_op(lambda e, sl2=sl2: e.tensor_tensor(out=xlnb[sl2], in0=dumpb,
                                                                in1=betrep, op=ALU.add))
                p.set_tok("dumpb.free", V=vxln)
                p.wait("scalar", "V", vxln)
                p.need("scalar", "dumpb.free")
                sss = sc_act(dumpb, xlnb[sl2], AF.Square, accum=sm_st[:, 9:10])
                p.set_tok("dumpb.free", S=sss)
                ch("scalar")
                snrm = sc_act(sm_st[:, 10:11], sm_st[:, 9:10], AF.Sqrt)
                p.wait("vector", "S", snrm)
                vt_op(lambda e: e.tensor_scalar(out=sm_st[:, 11:12], in0=sm_st[:, 10:11],
                                                scalar1=L2_EPS, scalar2=None, op0=ALU.max))
                ch("vector")
                vt_op(lambda e: e.reciprocal(out=sm_st[:, 12:13], in_=sm_st[:, 11:12]))
                ch("vector")
                vt_op(lambda e, tc=tc: e.tensor_tensor(out=sm_r[:, tc:tc + 1],
                                                       in0=sm_st[:, 12:13], in1=sm_scol,
                                                       op=ALU.mult))
                nsp2 = dma_out(d_xlnsp[tc * 128:(tc + 1) * 128, :], xlnb[sl2], f"D_stxb{sl2}")
                p.set_tok(f"xlnb{sl2}.free", **{f"D_stxb{sl2}": nsp2})
            p.mark("r.rdy", "V")
            p.barrier()
            if phases < 4:
                continue

            for half in range(2):
                for nci in range(8):
                    n_ = half * 8 + nci
                    sl = nci % 2
                    dma_in(wsb[sl], d_xlnsp[n_ * 128:(n_ + 1) * 128, :], f"wsbF{sl}",
                           sem=f"D_wsbF{sl}")
                    p.need("scalar", f"wsbF{sl}.rdy", "xnc.free", "r.rdy")
                    sxn = sc_act(xnc, wsb[sl], AF.Copy, scale=sm_r[:, n_:n_ + 1])
                    p.set_tok(f"wsbF{sl}.free", S=sxn)
                    for f4 in range(4):
                        b = PB[6 + f4 % 2]
                        p.need("tensor", bnm(b) + ".free")
                        p.wait("tensor", "S", sxn)
                        for q4 in range(4):
                            dc = f4 * 4 + q4
                            mm(b.ap()[:, q4 * 128:(q4 + 1) * 128],
                               xnc[:, dc * 128:(dc + 1) * 128], sm_id,
                               True, True, inc=(q4 == 3))
                        p.wait("scalar", "T", p.c("T"))
                        p.need("scalar", "xnT.free")
                        sev = sc_act(xnTr[:, f4 * 4:f4 * 4 + 4, nci * 128:(nci + 1) * 128],
                                     b.ap(), AF.Copy)
                        p.set_tok(bnm(b) + ".free", S=sev)
                    p.set_tok("xnc.free", T=p.c("T"))
                for esc in range(NE):
                    slm = esc % 2
                    stream2(d_mu[esc], wsb[slm], f"wsbF{slm}")
                    p.need("vector", f"wsbF{slm}.rdy", "musq.free")
                    vsq = vt_op(lambda e, slm=slm: e.tensor_tensor(out=musq, in0=wsb[slm],
                                                                   in1=wsb[slm], op=ALU.mult))
                    p.need("tensor", "pb4.free")
                    p.wait("tensor", "V", vsq)
                    for dc in range(NCH):
                        mm(PB[4].ap()[0:1, 0:128], sm_onc, musq[:, dc * 128:(dc + 1) * 128],
                           dc == 0, dc == NCH - 1, inc=(dc == NCH - 1))
                    p.mark("musq.free", "T")
                    p.wait("vector", "T", p.c("T"))
                    p.need("vector", "irow.free")
                    v1 = vt_op(lambda e: e.tensor_copy(out=irow, in_=PB[4].ap()[0:1, 0:128]))
                    p.set_tok("pb4.free", V=v1)
                    p.wait("scalar", "V", v1)
                    s1 = sc_act(irow2, irow, AF.Sqrt)
                    p.wait("vector", "S", s1)
                    ch("vector") if False else None
                    vt_op(lambda e: e.tensor_scalar(out=irow2, in0=irow2, scalar1=L2_EPS,
                                                    scalar2=None, op0=ALU.max))
                    ch("vector")
                    vinv = vt_op(lambda e: e.reciprocal(out=irow2, in_=irow2))
                    p.set_tok("irow.free", V=vinv)
                    p.need("tensor", "pb5.free")
                    p.wait("tensor", "V", vinv)
                    trp = mm(PB[5].ap()[:, 0:128], sm_onrf, irow2, True, True, inc=True)
                    p.wait("vector", "T", trp)
                    p.need("vector", "repb.free")
                    srp = vt_op(lambda e: e.tensor_copy(out=repb, in_=PB[5].ap()[:, 0:128]))
                    p.set_tok("pb5.free", V=srp)
                    ch("vector")
                    p.need("vector", "munb.free")
                    vmn = vt_op(lambda e, slm=slm: e.tensor_tensor(
                        out=munb, in0=wsb[slm],
                        in1=repb.unsqueeze(1).broadcast_to([128, 16, 128]),
                        op=ALU.mult))
                    p.set_tok("repb.free", V=vmn)
                    p.set_tok(f"wsbF{slm}.free", V=vmn)
                    for ng2 in range(2):
                        b = PB[(esc % 2) * 2 + ng2]
                        p.need("tensor", bnm(b) + ".free")
                        p.wait("tensor", "V", vmn)
                        for dc in range(NCH):
                            mm(b.ap(), munb[:, dc * 128:(dc + 1) * 128],
                               xnTr[:, dc, ng2 * 512:(ng2 + 1) * 512],
                               dc == 0, dc == NCH - 1, inc=(dc == NCH - 1))
                        p.wait("scalar", "T", p.c("T"))
                        p.need("scalar", "consts.rdy")
                        ci = esc * 4 + half * 2 + ng2
                        sE = sc_act(eTv[esc][:, half * 1024 + ng2 * 512:
                                             half * 1024 + (ng2 + 1) * 512],
                                    b.ap(), AF.Exp, bias=sm_negS,
                                    accum=sm_csp[:, ci:ci + 1])
                        p.set_tok(bnm(b) + ".free", S=sE)
                    p.set_tok("munb.free", T=p.c("T"))
                p.set_tok("xnT.free", T=p.c("T"))
            p.mark("eT.rdy", "S")
            for ng in range(4):
                p.need("tensor", "pb5.free", "eT.rdy")
                for esc in range(NE):
                    mm(PB[5].ap()[0:1, 0:512], sm_onc, eTv[esc][:, ng * 512:(ng + 1) * 512],
                       esc == 0, esc == NE - 1, inc=(esc == NE - 1))
                p.wait("vector", "T", p.c("T"))
                p.need("vector", "rowa.free")
                srs = vt_op(lambda e, ng=ng: e.tensor_copy(
                    out=row_a[:, ng * 512:(ng + 1) * 512], in_=PB[5].ap()[0:1, 0:512]))
                p.set_tok("pb5.free", V=srs)
            srsall = p.c("V")
            p.need("tensor", "pb6.free")
            p.wait("tensor", "V", srsall)
            for n_ in range(NCH):
                mm(PB[6].ap()[:, n_:n_ + 1], row_a[:, n_ * 128:(n_ + 1) * 128],
                   sm_onef, True, True, inc=(n_ == NCH - 1))
            trs = p.c("T")
            p.set_tok("rowa.free", T=trs)
            p.wait("vector", "T", trs)
            vrs = vt_op(lambda e: e.reciprocal(out=sm_rsi, in_=PB[6].ap()[:, 0:16]))
            p.set_tok("pb6.free", V=vrs)
            p.wait("vector", "S", p.c("S"))
            for esc in range(NE):
                vt_op(lambda e, esc=esc: e.tensor_reduce(out=sm_csc[:, esc:esc + 1],
                                                         in_=sm_csp[:, esc * 4:esc * 4 + 4],
                                                         axis=AX.X, op=ALU.add),
                      inc=(esc == NE - 1))
            ch("vector")
            vcs = vt_op(lambda e: e.reciprocal(out=sm_csi, in_=sm_csc))
            p.mark("csi.rdy", "V")
            p.barrier()
            if phases < 5:
                continue

            for n_ in range(NCH):
                dma_in(xlnv[n_], d_xlnsp[n_ * 128:(n_ + 1) * 128, :], f"xlnv{n_}",
                       sem="D_xlnv")
            xlq = p.c("D_xlnv")
            for e_ in EXPS:
                for f4 in range(4):
                    b = PB[6 + f4 % 2]
                    p.need("tensor", bnm(b) + ".free", f"Ee{f4}.free")
                    for q4 in range(4):
                        n_ = f4 * 4 + q4
                        mm(b.ap()[:, q4 * 128:(q4 + 1) * 128],
                           eTv[e_][:, n_ * 128:(n_ + 1) * 128], sm_id,
                           True, True, inc=(q4 == 3))
                    p.wait("scalar", "T", p.c("T"))
                    sev = 0
                    for q4 in range(4):
                        sev = sc_act(Eev[f4 * 4 + q4], b.ap()[:, q4 * 128:(q4 + 1) * 128],
                                     AF.Copy, inc=(q4 == 3))
                    p.set_tok(bnm(b) + ".free", S=sev)
                    p.set_tok(f"Ee{f4}.rdy", S=sev)
                p.need("tensor", "pb0.free", "pb1.free", "pb2.free", "pb3.free")
                p.wait("tensor", "D_xlnv", xlq)
                for n_ in range(NCH):
                    p.need("tensor", f"Ee{n_ // 4}.rdy")
                    for dg in range(4):
                        mm(PB[dg].ap(), Eev[n_], xlnv[n_][:, dg * 512:(dg + 1) * 512],
                           n_ == 0, n_ == NCH - 1, inc=(n_ == NCH - 1 and dg == 3))
                tsi = p.c("T")
                for f4 in range(4):
                    p.set_tok(f"Ee{f4}.free", T=tsi)
                p.wait("scalar", "T", tsi)
                p.need("scalar", "csi.rdy", "si.free")
                ssi = 0
                for dg in range(4):
                    ssi = sc_act(siv[:, dg * 512:(dg + 1) * 512], PB[dg].ap(), AF.Copy,
                                 scale=sm_csi[:, e_:e_ + 1], inc=(dg == 3))
                for dg in range(4):
                    p.set_tok(f"pb{dg}.free", S=ssi)
                for f4 in range(4):
                    b = PB[6 + f4 % 2]
                    p.need("tensor", bnm(b) + ".free")
                    p.wait("tensor", "S", ssi)
                    for q4 in range(4):
                        dc = f4 * 4 + q4
                        mm(b.ap()[:, q4 * 128:(q4 + 1) * 128],
                           siv[:, dc * 128:(dc + 1) * 128], sm_id, True, True, inc=(q4 == 3))
                    p.wait("scalar", "T", p.c("T"))
                    sev = 0
                    for q4 in range(4):
                        sev = sc_act(siTv[f4 * 4 + q4], b.ap()[:, q4 * 128:(q4 + 1) * 128],
                                     AF.Copy, inc=(q4 == 3))
                    p.set_tok(bnm(b) + ".free", S=sev)
                stiT = p.c("S")
                bq = dma_in(row_bf, d_b1[e_:e_ + 1, :], "rowbf")
                p.need("tensor", "pb6.free")
                p.wait("tensor", "D_rowbf", bq)
                trb = mm(PB[6].ap(), sm_onrf, row_bf, True, True, inc=True)
                p.set_tok("rowbf.free", T=trb)
                p.wait("vector", "T", trb)
                p.need("vector", "b1rep.free")
                vbr1 = vt_op(lambda e: e.tensor_copy(out=b1rv, in_=PB[6].ap()))
                p.set_tok("pb6.free", V=vbr1)
                hb = PB[4 + e_ % 2]
                p.need("tensor", bnm(hb) + ".free")
                for qd in range(4):
                    ap = d_w1[e_, qd * 512:(qd + 1) * 512, :].rearrange("(a p) h -> p a h", p=128)
                    slw = qd % 2
                    stream2(ap, wsb[slw], f"wsbF{slw}")
                    p.need("tensor", f"wsbF{slw}.rdy")
                    p.wait("tensor", "S", stiT)
                    for q4 in range(4):
                        dc = qd * 4 + q4
                        mm(hb.ap(), siTv[dc], wsb[slw][:, q4 * 512:(q4 + 1) * 512],
                           dc == 0, dc == 15, inc=(q4 == 3))
                    p.set_tok(f"wsbF{slw}.free", T=p.c("T"))
                th1 = p.c("T")
                p.set_tok("si.free", T=th1)
                p.wait("vector", "T", th1)
                p.wait("vector", "V", vbr1)
                p.need("vector", f"H{e_}.free")
                vhp = vt_op(lambda e, e_=e_, hb=hb: e.scalar_tensor_tensor(
                    out=Hv[e_], in0=hb.ap(), scalar=1.0, in1=b1rv,
                    op0=ALU.mult, op1=ALU.add))
                p.set_tok(bnm(hb) + ".free", V=vhp)
                p.set_tok("b1rep.free", V=vhp)
                p.wait("scalar", "V", vhp)
                sh = sc_act(Hv[e_], Hv[e_], AF.Gelu)
                p.set_tok(f"H{e_}.rdy", S=sh)
            p.barrier()
            if phases < 6:
                continue

            for e_ in EXPS:
                b = PB[6]
                p.need("tensor", "pb6.free", f"H{e_}.rdy")
                for q4 in range(4):
                    mm(b.ap()[:, q4 * 128:(q4 + 1) * 128],
                       Hv[e_][:, q4 * 128:(q4 + 1) * 128], sm_id, True, True, inc=(q4 == 3))
                tht = p.c("T")
                p.set_tok(f"H{e_}.free", T=tht)
                p.wait("scalar", "T", tht)
                p.need("scalar", "hT.free")
                shT = 0
                for q4 in range(4):
                    shT = sc_act(hTv[q4], b.ap()[:, q4 * 128:(q4 + 1) * 128], AF.Copy,
                                 inc=(q4 == 3))
                p.set_tok("pb6.free", S=shT)
                b2q = dma_in(row_a, d_b2[e_:e_ + 1, :], "rowa")
                p.wait("vector", "D_rowa", b2q)
                p.need("vector", "rowbb.free")
                vb2 = vt_op(lambda e: e.tensor_copy(out=row_bb, in_=row_a))
                p.set_tok("rowa.free", V=vb2)
                p.need("tensor", "pb0.free", "pb1.free", "pb2.free", "pb3.free")
                for hc in range(4):
                    slw = hc % 2
                    stream2(d_w2[e_, hc * 128:(hc + 1) * 128, :], wsb[slw], f"wsbF{slw}")
                    p.need("tensor", f"wsbF{slw}.rdy")
                    if hc == 0:
                        p.wait("tensor", "S", shT)
                    for dg in range(4):
                        mm(PB[dg].ap(), hTv[hc], wsb[slw][:, dg * 512:(dg + 1) * 512],
                           hc == 0, False, inc=(dg == 3))
                    p.set_tok(f"wsbF{slw}.free", T=p.c("T"))
                p.wait("tensor", "V", vb2)
                for dg in range(4):
                    mm(PB[dg].ap(), sm_onrb, row_bb[:, dg * 512:(dg + 1) * 512],
                       False, True, inc=(dg == 3))
                tso = p.c("T")
                p.set_tok("rowbb.free", T=tso)
                p.set_tok("hT.free", T=tso)
                p.wait("scalar", "T", tso)
                p.need("scalar", f"so{e_}.free")
                sso = 0
                for dg in range(4):
                    sso = sc_act(soutv[e_][:, dg * 512:(dg + 1) * 512], PB[dg].ap(), AF.Copy,
                                 inc=(dg == 3))
                for dg in range(4):
                    p.set_tok(f"pb{dg}.free", S=sso)
                p.set_tok(f"so{e_}.rdy", S=sso)
            p.barrier()
            if phases < 7:
                continue

            for n_ in range(NCH):
                grp = [PB[4 * (n_ % 2) + i] for i in range(4)]
                nx = dma_in(xrc, d_x1sp[n_ * 128:(n_ + 1) * 128, :], "r7")
                for dg in range(4):
                    b = grp[dg]
                    p.need("tensor", bnm(b) + ".free")
                    if n_ == 0 and dg == 0:
                        for e2 in EXPS:
                            p.need("tensor", f"so{e2}.rdy")
                    for esc in range(NE):
                        mm(b.ap(), eTv[esc][:, n_ * 128:(n_ + 1) * 128],
                           soutv[esc][:, dg * 512:(dg + 1) * 512],
                           esc == 0, esc == NE - 1, inc=(dg == 3 and esc == NE - 1))
                tcb = p.c("T")
                p.wait("vector", "T", tcb)
                p.need("vector", "r7.rdy")
                for dg in range(4):
                    vy2 = vt_op(lambda e, dg=dg, grp=grp, n_=n_: e.scalar_tensor_tensor(
                        out=xrc[:, dg * 512:(dg + 1) * 512], in0=grp[dg].ap(),
                        scalar=sm_rsi[:, n_:n_ + 1],
                        in1=xrc[:, dg * 512:(dg + 1) * 512], op0=ALU.mult, op1=ALU.add,
                        accum_out=sm_sc1[:, dg:dg + 1]), inc=(dg == 3))
                for dg in range(4):
                    p.set_tok(bnm(grp[dg]) + ".free", V=vy2)
                ch("vector")
                vs2 = vt_op(lambda e: e.tensor_reduce(out=sm_st[:, 0:1], in_=sm_sc1[:, 0:4],
                                                      axis=AX.X, op=ALU.add))
                p.wait("scalar", "V", vs2)
                rstd, bcol = stats_ln(xrc, 0)
                sout_ = sc_act(xrc, xrc, AF.Identity, bias=bcol, scale=rstd)
                ch("scalar")
                nsp = dma_out(d_out[el, n_ * 128:(n_ + 1) * 128, :], xrc, "D_str7")
                p.set_tok("r7.free", D_str7=nsp)

        for semname, cnt in sorted(p.counts.items()):
            if semname.startswith("D_") and cnt > 0:
                eng = "scalar" if semname in ("D_str7", "D_stxb0", "D_stxb1") else "sync"
                if semname == "D_dbg":
                    eng = "gpsimd"
                p.wait(eng, semname, cnt)

        sems = {s: E(nc.semaphore(s)) for s in sorted(p.counts)}

        block = E(nc.Block())

        @block.sync
        def _(e):
            p.replay("sync", e, sems)

        @block.tensor
        def _(e):
            p.replay("tensor", e, sems)

        @block.scalar
        def _(e):
            p.replay("scalar", e, sems)

        @block.vector
        def _(e):
            p.replay("vector", e, sems)

        @block.gpsimd
        def _(e):
            p.replay("gpsimd", e, sems)

    return nc


def _stage_weights(inputs):
    wq = np.asarray(inputs["L0_attn_wq"], np.float32)
    wq_r = np.ascontiguousarray(
        wq.reshape(16, 128, 16, 128).transpose(2, 1, 0, 3).reshape(16, 128, 2048))
    mu = np.asarray(inputs["L0_ffn_mu"], np.float32)
    mu_r = np.ascontiguousarray(
        mu.reshape(16, 128, 16, 128).transpose(2, 1, 0, 3).reshape(16, 128, 2048))
    w = dict(
        wq_r=wq_r,
        wk=np.ascontiguousarray(np.asarray(inputs["L0_attn_wk"], np.float32)),
        wv=np.ascontiguousarray(np.asarray(inputs["L0_attn_wv"], np.float32)),
        wo=np.ascontiguousarray(np.asarray(inputs["L0_attn_wo"], np.float32)),
        mu_r=mu_r,
        W1=np.ascontiguousarray(np.asarray(inputs["L0_ffn_W1"], np.float32)),
        W2=np.ascontiguousarray(np.asarray(inputs["L0_ffn_W2"], np.float32)),
        b1=np.ascontiguousarray(np.asarray(inputs["L0_ffn_b1"], np.float32)),
        b2=np.ascontiguousarray(np.asarray(inputs["L0_ffn_b2"], np.float32)),
        gamma=np.ascontiguousarray(np.asarray(inputs["L0_ffn_gamma"], np.float32).reshape(1, C)),
        beta=np.ascontiguousarray(np.asarray(inputs["L0_ffn_beta"], np.float32).reshape(1, C)),
        scale=np.ascontiguousarray(np.asarray(inputs["L0_ffn_scale"], np.float32).reshape(1, 1)),
    )
    w.update(host_constants())
    return w


def kernel(**inputs):
    x = np.asarray(inputs["L0_attn_x"], np.float32)
    w = _stage_weights(inputs)
    nc = build_program()
    in_maps = []
    for core in range(8):
        xs = np.ascontiguousarray(x[2 * core:2 * core + 2])
        xTs = np.ascontiguousarray(xs.transpose(0, 2, 1))
        m = dict(x=xs, xT=xTs)
        m.update(w)
        in_maps.append(m)
    res = run_bass_kernel_spmd(nc, in_maps, core_ids=list(range(8)))
    out = np.concatenate([np.asarray(r["out"]) for r in res.results], axis=0)
    return out
